# version 08
# speedup vs baseline: 1.1536x; 1.0089x over previous
import contextlib
import os
import numpy as np

import concourse.bass as bass
import concourse.mybir as mybir
from concourse.bass_utils import run_bass_kernel_spmd

F32 = mybir.dt.float32
BF16 = mybir.dt.bfloat16
AF = mybir.ActivationFunctionType
OP = mybir.AluOpType
AX = mybir.AxisListType

NCORES = 8
TFULL = 16384
TC = TFULL // NCORES
DM = 1024
NH, HD, M = 32, 32, 2
CK = 256
NCH = TC // CK
LN_EPS = 1e-5
DN_EPS = 1e-6
NTT = TC // 128
NT4 = TC // 512


ENGS = ("pe", "act", "dve", "pool", "sp")
DMAQS = ("spq", "poolq", "actq")
_QENG = {"spq": "sp", "poolq": "pool", "actq": "act"}


class P:
    def __init__(self, nc):
        self.nc = nc
        self.ops = {e: [] for e in ENGS}
        self.cnt = {e: 0 for e in ENGS}
        self.dcnt = {q: 0 for q in DMAQS}
        self.last_w = {}
        self.last_r = {}
        self.es = contextlib.ExitStack()
        self.waited = {}
        self.live_sb = {}
        self.alloc_guard = {}
        self.sb_bytes = 0

    def sb(self, name, shape, dtype):
        cm = self.nc.sbuf_tensor("sb_" + name, list(shape), dtype)
        t = cm.__enter__()
        self.live_sb[name] = cm
        self.sb_bytes += shape[-1] * mybir.dt.size(dtype) * 128
        if self.alloc_guard:
            d = self.last_r.setdefault(name, {})
            for prod, val in self.alloc_guard.items():
                d[prod] = max(d.get(prod, 0), val)
        return t

    def free(self, *names):
        order = [n for n in self.live_sb if n in names]
        assert set(order) == set(names), (names, list(self.live_sb))
        tail = list(self.live_sb)[-len(order):]
        assert tail == order, f"non-LIFO free: {order} vs live tail {tail}"
        for name in reversed(order):
            cm = self.live_sb.pop(name)
            cm.__exit__(None, None, None)
            for e in ENGS:
                if self.cnt[e]:
                    v = self.alloc_guard.get(e, 0)
                    self.alloc_guard[e] = max(v, self.cnt[e])
            for q in DMAQS:
                if self.dcnt[q]:
                    v = self.alloc_guard.get(q, 0)
                    self.alloc_guard[q] = max(v, self.dcnt[q])

    def ps(self, name, shape, dtype=F32):
        return self.es.enter_context(self.nc.psum_tensor(name, list(shape), dtype))

    def sem(self, name):
        return self.es.enter_context(self.nc.semaphore(name))

    def _need(self, reads, writes, eng):
        waits = {}
        def add(prod, val, same_ok):
            if prod == eng and prod == "pe":
                return
            if val > waits.get(prod, 0):
                waits[prod] = val
        for k in reads:
            w = self.last_w.get(k)
            if w is not None:
                add(w[0], w[1], True)
        for k in writes:
            w = self.last_w.get(k)
            if w is not None:
                add(w[0], w[1], True)
            for prod, val in self.last_r.get(k, {}).items():
                add(prod, val, False)
        return waits

    def _commit(self, eng_or_q, reads, writes, waits, fn, is_dma):
        if is_dma:
            q, eng = eng_or_q, _QENG[eng_or_q]
            prev = self.dcnt[q]
            if prev and prev > waits.get(q, 0):
                waits = dict(waits)
                waits[q] = prev
        else:
            q, eng = None, eng_or_q
        wt = self.waited.setdefault(eng, {})
        w2 = []
        for pr, v in waits.items():
            if v > wt.get(pr, 0):
                w2.append((pr, v))
                wt[pr] = v
        self.ops[eng].append((w2, fn, is_dma))
        if is_dma:
            self.dcnt[q] += 1
            prod, val = q, self.dcnt[q]
        else:
            self.cnt[eng] += 1
            prod, val = eng, self.cnt[eng]
        for k in writes:
            self.last_w[k] = (prod, val)
            self.last_r.pop(k, None)
        for k in reads:
            d = self.last_r.setdefault(k, {})
            d[prod] = max(d.get(prod, 0), val)

    def op(self, eng, fn, reads=(), writes=()):
        self._commit(eng, reads, writes, self._need(reads, writes, eng), fn,
                     False)

    def dma(self, q, out, in_, reads=(), writes=(), **kw):
        def fn(eng, out=out, in_=in_, kw=kw):
            return eng.dma_start(out=out, in_=in_, **kw)
        self._commit(q, reads, writes,
                     self._need(reads, writes, _QENG[q]), fn, True)

    def dump(self, name, ap, reads):
        shp = list(ap.shape)
        d = self.nc.dram_tensor("dump_" + name, shp, ap.dtype,
                                kind="ExternalOutput").ap()
        self.dma("spq", d, ap, reads=reads, writes=("dumpd_" + name,))

    def finalize(self):
        nc = self.nc
        prog = {e: self.sem("prog_" + e) for e in ENGS}
        dsem = {q: self.sem("dma_" + q) for q in DMAQS}

        def sem_of(prod):
            return (dsem[prod], 16) if prod in DMAQS else (prog[prod], 1)

        block = self.es.enter_context(nc.Block())
        engmap = {"sp": block.sync, "pool": block.gpsimd, "pe": block.tensor,
                  "dve": block.vector, "act": block.scalar}

        def make_runner(ename):
            my_dmaq = [q for q, e in _QENG.items() if e == ename]
            dq = my_dmaq[0] if my_dmaq else None

            def runner(eng):
                for waits, fn, is_dma in self.ops[ename]:
                    for prod, val in waits:
                        s, mult = sem_of(prod)
                        eng.wait_ge(s, val * mult)
                    inst = fn(eng)
                    if is_dma:
                        inst.then_inc(dsem[dq], 16)
                    else:
                        inst.then_inc(prog[ename], 1)
                if ename == "sp":
                    for q in DMAQS:
                        if self.dcnt[q]:
                            eng.wait_ge(dsem[q], 16 * self.dcnt[q])
                    for e in ENGS:
                        if e != "sp" and self.cnt[e]:
                            eng.wait_ge(prog[e], self.cnt[e])
            return runner

        for ename, deco in engmap.items():
            deco(make_runner(ename))
        for name in reversed(list(self.live_sb)):
            self.live_sb.pop(name).__exit__(None, None, None)
        self.es.close()



def host_tables(core):
    t0 = core * TC
    tl = {}
    tl["ident"] = np.eye(128, dtype=np.float32)
    half = HD // 2
    invf = (1.0 / 10000.0 ** (np.arange(half) / half)).astype(np.float64)
    tt = (t0 + np.arange(TC)).astype(np.float64)
    ang = tt[None, :] * invf[:, None]
    tl["ropec"] = np.tile(np.cos(ang), (8, 1)).astype(np.float32)
    tl["ropes"] = np.tile(np.sin(ang), (8, 1)).astype(np.float32)
    Pm = np.zeros((128, 128), np.float32)
    for h in range(4):
        for d in range(16):
            Pm[32 * h + d + 16, 32 * h + d] = -1.0
            Pm[32 * h + d, 32 * h + d + 16] = 1.0
    tl["ropeP"] = Pm
    tl["dn_mask"] = (np.arange(256)[None, :] >= np.arange(128)[:, None]
                     ).astype(np.float32)
    Bl = np.zeros((M, 8, 64, 128), np.float32)
    Bd = np.zeros((M, 8, 128, 64), np.float32)
    for m in range(M):
        for ph in range(8):
            for r in range(128):
                h = ph * 4 + r // 32
                Bl[m, ph, 2 * h + m, r] = 1.0
                Bd[m, ph, r, 2 * h + m] = 1.0
    tl["Blam"] = np.ascontiguousarray(
        Bl.transpose(2, 0, 1, 3).reshape(64, M * 8 * 128))
    tl["Bden"] = np.ascontiguousarray(
        Bd.transpose(2, 0, 1, 3).reshape(128, M * 8 * 64))
    tl["ltmask"] = (np.arange(8) < core).astype(np.float32).reshape(1, 8)
    return tl



def emit_weight_T_tiles(p, dram_ap, raw, raw_key, tiles, tiles_key, ident,
                        psb, psb_key, nchunks, kchunks, rows=128):
    for nc_i in range(nchunks):
        p.dma("poolq", raw[0:rows, :], dram_ap[nc_i * rows:(nc_i + 1) * rows, :],
              reads=(raw_key,), writes=(raw_key,))
        for kc in range(kchunks):
            def tf(eng, kc=kc):
                return eng.transpose(psb[:, 0:rows],
                                     raw[0:rows, kc * 128:(kc + 1) * 128],
                                     ident[0:rows, 0:rows])
            p.op("pe", tf, reads=(raw_key, "ident"), writes=(psb_key,))
            def ef(eng, kc=kc, nc_i=nc_i):
                return eng.tensor_copy(
                    tiles[:, (kc * nchunks + nc_i) * rows:
                          (kc * nchunks + nc_i + 1) * rows],
                    psb[0:128, 0:rows])
            p.op("dve", ef, reads=(psb_key,), writes=(tiles_key,))


def emit_ln_x_to_xnfm(p, x_dram, xn, psb, ident, pref="x0", xn_key="xn",
                      sqps=None, sqps_key="psA"):
    xt = p.sb(pref + "_xt", [128, DM], F32)
    mu = p.sb(pref + "_mu", [128, 1], F32)
    ss = p.sb(pref + "_ss", [128, 2], F32)
    msq = p.sb(pref + "_msq", [128, 1], F32)
    rstd = p.sb(pref + "_rs", [128, 1], F32)
    xnt = p.sb(pref + "_xnt", [128, DM], BF16)
    names = [pref + x for x in ("_xt", "_mu", "_ss", "_msq", "_rs", "_xnt")]
    for tt in range(NTT):
        p.dma("spq", xt[:, :], x_dram[tt * 128:(tt + 1) * 128, :],
              reads=("xt" + pref,), writes=("xt" + pref,))
        for hf in range(2):
            p.op("act", lambda e, hf=hf: e.activation(
                sqps[:, 0:512], xt[:, hf * 512:(hf + 1) * 512], AF.Square,
                accum_out=ss[:, hf:hf + 1]),
                reads=("xt" + pref,), writes=(sqps_key, "ss" + pref))
        p.op("dve", lambda e: e.tensor_add(ss[:, 0:1], ss[:, 0:1], ss[:, 1:2]),
             reads=("ss" + pref,), writes=("ss" + pref,))
        p.op("dve", lambda e: e.reduce_sum(mu[:, 0:1], xt[:, :], AX.X),
             reads=("xt" + pref,), writes=("mu" + pref,))
        p.op("dve", lambda e: e.tensor_scalar_mul(mu[:, 0:1], mu[:, 0:1], 1.0 / DM),
             reads=("mu" + pref,), writes=("mu" + pref,))
        p.op("dve", lambda e: e.tensor_mul(msq[:, 0:1], mu[:, 0:1], mu[:, 0:1]),
             reads=("mu" + pref,), writes=("msq" + pref,))
        p.op("dve", lambda e: e.tensor_scalar(rstd[:, 0:1], ss[:, 0:1],
                                              1.0 / DM, LN_EPS, OP.mult, OP.add),
             reads=("ss" + pref,), writes=("rstd" + pref,))
        p.op("dve", lambda e: e.tensor_sub(rstd[:, 0:1], rstd[:, 0:1], msq[:, 0:1]),
             reads=("rstd" + pref, "msq" + pref), writes=("rstd" + pref,))
        p.op("act", lambda e: e.activation(rstd[:, 0:1], rstd[:, 0:1], AF.Sqrt),
             reads=("rstd" + pref,), writes=("rstd" + pref,))
        p.op("dve", lambda e: e.reciprocal(rstd[:, 0:1], rstd[:, 0:1]),
             reads=("rstd" + pref,), writes=("rstd" + pref,))
        p.op("dve", lambda e: e.tensor_scalar(xnt[:, :], xt[:, :], mu[:, 0:1],
                                              rstd[:, 0:1], OP.subtract, OP.mult),
             reads=("xt" + pref, "mu" + pref, "rstd" + pref),
             writes=("xnt" + pref,))
        for dc in range(8):
            p.op("pe", lambda e, dc=dc: e.transpose(
                psb[:, 0:128], xnt[:, dc * 128:(dc + 1) * 128], ident[:, :]),
                reads=("xnt" + pref, "ident"), writes=("psb_ln",))
            p.op("dve", lambda e, dc=dc, tt=tt: e.tensor_copy(
                xn[:, dc * TC + tt * 128: dc * TC + (tt + 1) * 128],
                psb[:, 0:128]),
                reads=("psb_ln",), writes=(xn_key,))
    return names


def emit_mm_acc(p, psum_ap, ps_key, pairs, tile_position=None):
    n = len(pairs)
    for i, (lap, lkey, rap, rkey) in enumerate(pairs):
        def mf(eng, lap=lap, rap=rap, i=i):
            return eng.matmul(psum_ap, lap, rap, start=(i == 0),
                              stop=(i == n - 1), tile_position=tile_position)
        p.op("pe", mf, reads=(lkey, rkey), writes=(ps_key,))



def build_l0a(debug=False):
    nc = bass.Bass("TRN2", target_bir_lowering=False, debug=False)
    p = P(nc)

    x_d = nc.dram_tensor("x", [TC, DM], F32, kind="ExternalInput").ap()
    wk_d = nc.dram_tensor("Wk", [DM, DM], F32, kind="ExternalInput").ap()
    wv_d = nc.dram_tensor("Wv", [DM, DM], F32, kind="ExternalInput").ap()
    wb_d = nc.dram_tensor("Wbeta", [64, DM], F32, kind="ExternalInput").ap()
    bb_d = nc.dram_tensor("bbeta", [64, 1], F32, kind="ExternalInput").ap()
    identd = nc.dram_tensor("ident", [128, 128], F32, kind="ExternalInput").ap()
    ropec_d = nc.dram_tensor("ropec", [128, TC], F32, kind="ExternalInput").ap()
    ropes_d = nc.dram_tensor("ropes", [128, TC], F32, kind="ExternalInput").ap()
    ropep_d = nc.dram_tensor("ropeP", [128, 128], F32, kind="ExternalInput").ap()
    blam_d = nc.dram_tensor("Blam", [64, M * 8 * 128], F32,
                            kind="ExternalInput").ap()
    state_d = nc.dram_tensor("state", [128, 529], F32, kind="ExternalOutput").ap()

    ident = p.sb("identb", [128, 128], BF16)
    xn = p.sb("xn", [128, 8 * TC], BF16)
    w_tiles = p.sb("w_tiles", [128, 64 * 128], BF16)
    wk_tiles = w_tiles
    wv_tiles = w_tiles
    wb_tiles = p.sb("wb_tiles", [128, 8 * 64], BF16)
    wraw = p.sb("wraw", [128, DM], BF16)
    bbeta = p.sb("bbeta", [64, 1], F32)
    ropec = p.sb("ropec", [128, TC], F32)
    ropes = p.sb("ropes", [128, TC], F32)
    ropeP = p.sb("ropeP", [128, 128], BF16)
    blamf = p.sb("blamf", [64, M * 8 * 128], F32)
    phik = p.sb("phik", [128, 8 * TC], BF16)
    kwork = p.sb("kwork", [128, 512], F32)
    kb16 = p.sb("kb16", [128, 512], BF16)
    kswap = p.sb("kswap", [128, 512], F32)
    vt = p.sb("vt", [128, NTT * DM], BF16)
    beta = p.sb("beta", [64, TC], F32)
    lam = p.sb("lam", [64, TC], F32)
    rlam = p.sb("rlam", [64, TC], F32)
    lamc = p.sb("lamc", [64, 8], F32)
    lamcum = p.sb("lamcum", [64, 9], F32)
    fac = p.sb("fac", [64, CK], F32)
    khat = p.sb("khat", [128, 8 * CK], BF16)
    khat_tm = p.sb("khat_tm", [128, 2 * DM], BF16)
    ones_col_b = p.sb("ones_col_b", [128, 1], BF16)
    S_sb = p.sb("S_sb", [128, 16 * 32], F32)
    K_sb = p.sb("K_sb", [128, 16], F32)
    lamcol = p.sb("lamcol", [128, 16 * 8], F32)
    st_out = p.sb("st_out", [128, 529], F32)

    psb1 = p.ps("psb1", [128, 1024], BF16)
    psb2 = p.ps("psb2", [128, 1024], BF16)
    psA = p.ps("psA", [128, 512], F32)
    psB = p.ps("psB", [128, 512], F32)
    psC = p.ps("psC", [128, 512], F32)
    psD = p.ps("psD", [128, 512], F32)
    psE = p.ps("psE", [128, 512], F32)
    psF = p.ps("psF", [128, 512], F32)

    p.dma("poolq", ident[:, :], identd, writes=("ident",))
    p.dma("poolq", ropeP[:, :], ropep_d, writes=("ropeP",))
    p.dma("spq", ropec[:, :], ropec_d, writes=("ropec",))
    p.dma("spq", ropes[:, :], ropes_d, writes=("ropes",))
    p.dma("spq", bb_to := bbeta[:, :], bb_d, writes=("bbeta",))
    p.dma("spq", blamf[:, :], blam_d, writes=("blamf",))
    p.op("dve", lambda e: e.memset(ones_col_b[:, :], 1.0), writes=("ones_col_b",))

    emit_weight_T_tiles(p, wk_d, wraw, "wraw", wk_tiles, "w_tiles", ident,
                        psb1, "psb1", 8, 8)
    wbraw = p.sb("wbraw", [64, DM], BF16)
    p.dma("poolq", wbraw[:, :], wb_d, writes=("wbraw",))
    for kc in range(8):
        p.op("pe", lambda e, kc=kc: e.transpose(
            psb1[:, 0:64], wbraw[0:64, kc * 128:(kc + 1) * 128],
            ident[0:64, 0:64]),
            reads=("wbraw", "ident"), writes=("psb1",))
        p.op("dve", lambda e, kc=kc: e.tensor_copy(
            wb_tiles[:, kc * 64:(kc + 1) * 64], psb1[:, 0:64]),
            reads=("psb1",), writes=("wb_tiles",))

    emit_ln_x_to_xnfm(p, x_d, xn, psb2, ident, sqps=psA, sqps_key="psA")

    for dc in range(8):
        for t4 in range(NT4):
            tsl = slice(t4 * 512, (t4 + 1) * 512)
            osl = slice(dc * TC + t4 * 512, dc * TC + (t4 + 1) * 512)
            pairs = [(wk_tiles[:, (kc * 8 + dc) * 128:(kc * 8 + dc + 1) * 128],
                      "w_tiles",
                      xn[:, kc * TC + t4 * 512: kc * TC + (t4 + 1) * 512], "xn")
                     for kc in range(8)]
            emit_mm_acc(p, psA[:, :], "psA", pairs)
            p.op("dve", lambda e: e.tensor_copy(kwork[:, :], psA[:, :]),
                 reads=("psA",), writes=("kwork",))
            p.op("dve", lambda e: e.tensor_copy(kb16[:, :], kwork[:, :]),
                 reads=("kwork",), writes=("kb16",))
            p.op("pe", lambda e: e.matmul(psD[:, :], ropeP[:, :], kb16[:, :],
                                          start=True, stop=True),
                 reads=("kb16", "ropeP"), writes=("psD",))
            p.op("dve", lambda e, tsl=tsl: e.tensor_mul(
                kwork[:, :], kwork[:, :], ropec[:, tsl]),
                reads=("kwork", "ropec"), writes=("kwork",))
            p.op("dve", lambda e, tsl=tsl: e.tensor_mul(
                kswap[:, :], psD[:, :], ropes[:, tsl]),
                reads=("psD", "ropes"), writes=("kswap",))
            p.op("dve", lambda e: e.tensor_add(
                kwork[:, :], kwork[:, :], kswap[:, :]),
                reads=("kwork", "kswap"), writes=("kwork",))
            p.op("dve", lambda e: e.tensor_scalar_min(
                kswap[:, :], kwork[:, :], 0.0),
                reads=("kwork",), writes=("kswap",))
            p.op("act", lambda e: e.activation(kswap[:, :], kswap[:, :], AF.Exp),
                 reads=("kswap",), writes=("kswap",))
            p.op("dve", lambda e: e.tensor_relu(kwork[:, :], kwork[:, :]),
                 reads=("kwork",), writes=("kwork",))
            p.op("dve", lambda e, osl=osl: e.tensor_add(
                phik[:, osl], kwork[:, :], kswap[:, :]),
                reads=("kwork", "kswap"), writes=("phik",))
    for t4 in range(NT4):
        pairs = [(wb_tiles[:, kc * 64:(kc + 1) * 64], "wb_tiles",
                  xn[:, kc * TC + t4 * 512: kc * TC + (t4 + 1) * 512], "xn")
                 for kc in range(8)]
        emit_mm_acc(p, psB[0:64, :], "psB", pairs)
        p.op("act", lambda e, t4=t4: e.activation(
            beta[:, t4 * 512:(t4 + 1) * 512], psB[0:64, :], AF.Sigmoid,
            bias=bbeta[:, 0:1]),
            reads=("psB", "bbeta"), writes=("beta",))
        p.op("dve", lambda e, t4=t4: e.tensor_scalar(
            beta[:, t4 * 512:(t4 + 1) * 512], beta[:, t4 * 512:(t4 + 1) * 512],
            0.85, 0.9995, OP.max, OP.min),
            reads=("beta",), writes=("beta",))
    emit_weight_T_tiles(p, wv_d, wraw, "wraw", wv_tiles, "w_tiles", ident,
                        psb1, "psb1", 8, 8)
    for tt in range(NTT):
        for half in range(2):
            for j in range(4):
                pairs = [(xn[:, kc * TC + tt * 128: kc * TC + (tt + 1) * 128],
                          "xn",
                          wv_tiles[:, (kc * 8 + half * 4 + j) * 128:
                                   (kc * 8 + half * 4 + j + 1) * 128],
                          "w_tiles") for kc in range(8)]
                emit_mm_acc(p, psC[:, j * 128:(j + 1) * 128], "psC_" + str(j),
                            pairs)
            p.op("dve", lambda e, tt=tt, half=half: e.tensor_copy(
                vt[:, tt * DM + half * 512: tt * DM + (half + 1) * 512],
                psC[:, :]),
                reads=("psC_0", "psC_1", "psC_2", "psC_3"), writes=("vt",))

    for c in range(NCH):
        sl = slice(c * CK, (c + 1) * CK)
        p.op("dve", lambda e, sl=sl: e.tensor_tensor_scan(
            lam[:, sl], beta[:, sl], beta[:, sl], 1.0, OP.mult, OP.bypass),
            reads=("beta",), writes=("lam",))
    p.op("dve", lambda e: e.reciprocal(rlam[:, :], lam[:, :]),
         reads=("lam",), writes=("rlam",))
    p.op("dve", lambda e: e.tensor_copy(lamc[:, :], lam[:, CK - 1: TC: CK]),
         reads=("lam",), writes=("lamc",))
    p.op("dve", lambda e: e.memset(lamcum[:, 0:1], 1.0), writes=("lamcum",))
    p.op("dve", lambda e: e.tensor_tensor_scan(
        lamcum[:, 1:9], lamc[:, :], lamc[:, :], 1.0, OP.mult, OP.bypass),
        reads=("lamc",), writes=("lamcum",))

    for m in range(M):
        for ph in range(8):
            i = m * 8 + ph
            p.op("pe", lambda e, i=i: e.matmul(
                psE[:, 0:8], blamf[:, i * 128:(i + 1) * 128], lamc[:, :],
                start=True, stop=True),
                reads=("blamf", "lamc"), writes=("psE",))
            p.op("dve", lambda e, i=i: e.tensor_copy(
                lamcol[:, i * 8:(i + 1) * 8], psE[:, 0:8]),
                reads=("psE",), writes=("lamcol",))

    p.op("dve", lambda e: e.memset(S_sb[:, :], 0.0), writes=("S_sb",))
    p.op("dve", lambda e: e.memset(K_sb[:, :], 0.0), writes=("K_sb",))
    emit_state_chunks(p, dict(phik=phik, vt=vt, lamc=lamc, blamf=blamf,
                              rlam_ap=lambda c: rlam[:, c * CK:(c + 1) * CK],
                              rlam_key="rlam",
                              fac=fac, khat=khat, khat_tm=khat_tm, ident=ident,
                              ones_col_b=ones_col_b, S_sb=S_sb, K_sb=K_sb,
                              lamcol=lamcol, psb1=psb1, psE=psE, psF=psF))

    if debug:
        p.dump("xn0", xn[:, 0:2048], ("xn",))
        p.dump("beta", beta[:, :], ("beta",))
        p.dump("lam", lam[:, :], ("lam",))
        p.dump("lamc", lamc[:, :], ("lamc",))
        p.dump("lamcum", lamcum[:, :], ("lamcum",))
        p.dump("phik0", phik[:, 0:2048], ("phik",))
        p.dump("vt0", vt[:, 0:1024], ("vt",))
        p.dump("khat", khat[:, :], ("khat",))
        p.dump("khat_tm", khat_tm[:, :], ("khat_tm",))
        p.dump("lamcol", lamcol[:, :], ("lamcol",))

    p.op("dve", lambda e: e.tensor_copy(st_out[:, 0:512], S_sb[:, :]),
         reads=("S_sb",), writes=("st_out",))
    p.op("dve", lambda e: e.tensor_copy(st_out[:, 512:528], K_sb[:, :]),
         reads=("K_sb",), writes=("st_out",))
    p.op("dve", lambda e: e.memset(st_out[:, 528:529], 0.0), writes=("st_out",))
    p.op("dve", lambda e: e.tensor_copy(st_out[0:64, 528:529], lamcum[:, 8:9]),
         reads=("lamcum",), writes=("st_out",))
    p.dma("spq", state_d, st_out[:, :], reads=("st_out",), writes=("state_d",))

    p.finalize()
    return nc


def emit_state_chunks(p, t):
    for c in range(NCH):
        emit_state_chunk(p, t, c)


def emit_state_chunk(p, t, c):
    kf, vt = t["phik"], t["vt"]
    if True:
        for m in range(M):
            p.op("dve", lambda e, c=c: e.tensor_scalar(
                t["fac"][:, :], t["rlam_ap"](c),
                t["lamc"][:, c:c + 1], None, OP.mult),
                reads=(t["rlam_key"], "lamc"), writes=("fac",))
            for ph in range(8):
                i = m * 8 + ph
                p.op("pe", lambda e, i=i: e.matmul(
                    psE_ap(t)[:, 0:CK], t["blamf"][:, i * 128:(i + 1) * 128],
                    t["fac"][:, :], start=True, stop=True),
                    reads=("blamf", "fac"), writes=("psE",))
                p.op("dve", lambda e, ph=ph, c=c: e.tensor_mul(
                    t["khat"][:, ph * CK:(ph + 1) * CK],
                    kf[:, ph * TC + c * CK: ph * TC + (c + 1) * CK],
                    psE_ap(t)[:, 0:CK]),
                    reads=("phik", "psE"), writes=("khat",))
            for ph in range(8):
                for ts in range(2):
                    p.op("pe", lambda e, ph=ph, ts=ts: e.transpose(
                        t["psb1"][:, 0:128],
                        t["khat"][:, ph * CK + ts * 128: ph * CK + (ts + 1) * 128],
                        t["ident"][:, :]),
                        reads=("khat", "ident"), writes=("psb1",))
                    p.op("dve", lambda e, ph=ph, ts=ts: e.tensor_copy(
                        t["khat_tm"][:, ts * DM + ph * 128:
                                     ts * DM + (ph + 1) * 128],
                        t["psb1"][:, 0:128]),
                        reads=("psb1",), writes=("khat_tm",))
            for ph in range(8):
                for ts in range(2):
                    ltap = t["khat_tm"][:, ts * DM + ph * 128:
                                        ts * DM + (ph + 1) * 128]
                    tti = c * 2 + ts
                    p.op("pe", lambda e, ltap=ltap, tti=tti, ph=ph, ts=ts:
                         e.matmul(t["psF"][:, 0:128], ltap,
                                  vt[:, tti * DM + ph * 128:
                                     tti * DM + (ph + 1) * 128],
                                  start=(ts == 0), stop=(ts == 1)),
                         reads=("khat_tm", "vt"), writes=("psF",))
                for ts in range(2):
                    ltap = t["khat_tm"][:, ts * DM + ph * 128:
                                        ts * DM + (ph + 1) * 128]
                    p.op("pe", lambda e, ltap=ltap, ts=ts: e.matmul(
                        t["psF"][:, 128:129], ltap, t["ones_col_b"][:, :],
                        start=(ts == 0), stop=(ts == 1)),
                        reads=("khat_tm", "ones_col_b"), writes=("psF",))
                pc = m * 8 + ph
                lcol = t["lamcol"]
                for j in range(4):
                    p.op("dve", lambda e, pc=pc, j=j, c=c:
                         e.scalar_tensor_tensor(
                             t["S_sb"][32 * j:32 * (j + 1), pc * 32:(pc + 1) * 32],
                             t["S_sb"][32 * j:32 * (j + 1), pc * 32:(pc + 1) * 32],
                             lcol[32 * j:32 * (j + 1), pc * 8 + c: pc * 8 + c + 1],
                             t["psF"][32 * j:32 * (j + 1), 32 * j:32 * (j + 1)],
                             OP.mult, OP.add),
                         reads=("psF", "lamcol", "S_sb"), writes=("S_sb",))
                p.op("dve", lambda e, pc=pc, c=c: e.scalar_tensor_tensor(
                    t["K_sb"][:, pc:pc + 1], t["K_sb"][:, pc:pc + 1],
                    lcol[:, pc * 8 + c: pc * 8 + c + 1],
                    t["psF"][:, 128:129], OP.mult, OP.add),
                    reads=("psF", "lamcol", "K_sb"), writes=("K_sb",))


def psE_ap(t):
    return t["psE"]



_L0A_CACHE = [None]


def run_l0a(inputs):
    x = np.ascontiguousarray(np.asarray(inputs["L0_attn_x"]).reshape(TFULL, DM))
    if _L0A_CACHE[0] is None:
        _L0A_CACHE[0] = build_l0a(debug=bool(int(
            os.environ.get("KDEBUG", "0"))))
    nc = _L0A_CACHE[0]
    in_maps = []
    for c in range(NCORES):
        tl = host_tables(c)
        in_maps.append({
            "x": np.ascontiguousarray(x[c * TC:(c + 1) * TC]),
            "Wk": np.asarray(inputs["L0_attn_Wk"], np.float32),
            "Wv": np.asarray(inputs["L0_attn_Wv"], np.float32),
            "Wbeta": np.asarray(inputs["L0_attn_Wbeta"], np.float32),
            "bbeta": np.ascontiguousarray(
                np.asarray(inputs["L0_attn_bbeta"], np.float32).reshape(64, 1)),
            "ident": tl["ident"],
            "ropec": tl["ropec"],
            "ropes": tl["ropes"],
            "ropeP": tl["ropeP"],
            "Blam": tl["Blam"],
        })
    res = run_bass_kernel_spmd(nc, in_maps, core_ids=list(range(NCORES)))
    run_l0a.last = res
    return np.stack([res.results[c]["state"] for c in range(NCORES)])



def emit_proj_rope_phi(p, tiles, tiles_key, xn, psA, psD, ropeP, ropec, ropes,
                       kwork, kb16, kswap, out, out_key, rope_dma=None):
    for t4 in range(NT4):
        if rope_dma is not None:
            rope_dma(t4)
        for dc in range(8):
            tsl = slice(0, 512)
            osl = slice(dc * TC + t4 * 512, dc * TC + (t4 + 1) * 512)
            pairs = [(tiles[:, (kc * 8 + dc) * 128:(kc * 8 + dc + 1) * 128],
                      tiles_key,
                      xn[:, kc * TC + t4 * 512: kc * TC + (t4 + 1) * 512], "xn")
                     for kc in range(8)]
            emit_mm_acc(p, psA[:, :], "psA", pairs)
            p.op("dve", lambda e: e.tensor_copy(kwork[:, :], psA[:, :]),
                 reads=("psA",), writes=("kwork",))
            p.op("dve", lambda e: e.tensor_copy(kb16[:, :], kwork[:, :]),
                 reads=("kwork",), writes=("kb16",))
            p.op("pe", lambda e: e.matmul(psD[:, :], ropeP[:, :], kb16[:, :],
                                          start=True, stop=True),
                 reads=("kb16", "ropeP"), writes=("psD",))
            p.op("dve", lambda e, tsl=tsl: e.tensor_mul(
                kwork[:, :], kwork[:, :], ropec[:, tsl]),
                reads=("kwork", "ropec"), writes=("kwork",))
            p.op("dve", lambda e, tsl=tsl: e.tensor_mul(
                kswap[:, :], psD[:, :], ropes[:, tsl]),
                reads=("psD", "ropes"), writes=("kswap",))
            p.op("dve", lambda e: e.tensor_add(
                kwork[:, :], kwork[:, :], kswap[:, :]),
                reads=("kwork", "kswap"), writes=("kwork",))
            p.op("dve", lambda e: e.tensor_scalar_min(
                kswap[:, :], kwork[:, :], 0.0),
                reads=("kwork",), writes=("kswap",))
            p.op("act", lambda e: e.activation(kswap[:, :], kswap[:, :], AF.Exp),
                 reads=("kswap",), writes=("kswap",))
            p.op("dve", lambda e: e.tensor_relu(kwork[:, :], kwork[:, :]),
                 reads=("kwork",), writes=("kwork",))
            p.op("dve", lambda e, osl=osl: e.tensor_add(
                out[:, osl], kwork[:, :], kswap[:, :]),
                reads=("kwork", "kswap"), writes=(out_key,))


def build_l0b(debug=False):
    nc = bass.Bass("TRN2", target_bir_lowering=False, debug=False)
    p = P(nc)

    def DI(n, shp):
        return nc.dram_tensor(n, shp, F32, kind="ExternalInput").ap()

    x_d = DI("x", [TC, DM])
    wq_d = DI("Wq", [DM, DM])
    wk_d = DI("Wk", [DM, DM])
    wv_d = DI("Wv", [DM, DM])
    wb_d = DI("Wbeta", [64, DM])
    bb_d = DI("bbeta", [64, 1])
    wm_d = DI("Wmix", [64, DM])
    bm_d = DI("bmix", [64, 1])
    wo_d = DI("Wout", [DM, DM])
    bo_d = DI("bout", [128, 8])
    lng_d = DI("ln_g", [128, 8])
    lnb_d = DI("ln_b", [128, 8])
    gw_d = DI("gate_w", [DM, 8])
    gb_d = DI("gate_b", [8, 1])
    ew_d = DI("expert_w", [8 * 128, 64 * 128])
    eb_d = DI("expert_b", [8, DM])
    st8_d = DI("states8", [128, 8 * 529])
    ltm_d = DI("ltmask", [1, 8])
    identd = DI("ident", [128, 128])
    ropec_d = DI("ropec", [128, TC])
    ropes_d = DI("ropes", [128, TC])
    ropep_d = DI("ropeP", [128, 128])
    blam_d = DI("Blam", [64, M * 8 * 128])
    bden_d = DI("Bden", [128, M * 8 * 64])
    bsel8_d = DI("bsel8", [8, 8 * 128])
    dnm_d = DI("dn_mask", [128, 256])
    x1_d = nc.dram_tensor("x1", [DM, TC], F32, kind="ExternalOutput").ap()
    xnsp_d = nc.dram_tensor("xn_spill", [128, 8 * TC], BF16).ap()
    ysp_d = nc.dram_tensor("y_spill", [128, 8 * TC], BF16).ap()
    zsp_d = nc.dram_tensor("z_spill", [128, 8 * TC], F32).ap()

    ident = p.sb("identb", [128, 128], BF16)
    identf = p.sb("identf", [128, 128], F32)
    ropeP = p.sb("ropeP", [128, 128], BF16)
    blamf = p.sb("blamf", [64, M * 8 * 128], F32)
    bdenb = p.sb("bdenb", [128, M * 8 * 64], BF16)
    bsel8 = p.sb("bsel8", [8, 8 * 128], BF16)
    dnm = p.sb("dnm", [128, 256], BF16)
    onescb = p.sb("onescb", [128, 1], BF16)
    onesrf = p.sb("onesrf", [1, 128], F32)
    vecs = p.sb("vecs", [128, 4], F32)
    boutc = p.sb("boutc", [128, 8], F32)
    lngc = p.sb("lngc", [128, 8], F32)
    lnbc = p.sb("lnbc", [128, 8], F32)
    gbc = p.sb("gbc", [8, 1], F32)
    ebt = p.sb("ebt", [8, DM], BF16)
    w_tiles = p.sb("w_tiles", [128, 64 * 128], BF16)
    wbm_tiles = p.sb("wbm_tiles", [128, 8 * 128], BF16)
    gwt = p.sb("gwt", [128, 8 * 8], BF16)
    wraw = p.sb("wraw", [128, DM], BF16)
    beta = p.sb("beta", [64, TC], F32)
    lam = p.sb("lam", [64, TC], F32)
    lamr = p.sb("lamr", [64, CK], F32)
    lamc = p.sb("lamc", [64, 8], F32)
    mixl = p.sb("mixl", [64, TC], BF16)
    sden = p.sb("sden", [64, TC], F32)
    S_sb = p.sb("S_sb", [128, 16 * 32], F32)
    K_sb = p.sb("K_sb", [128, 16], F32)
    lamcol = p.sb("lamcol", [128, 16 * 8], F32)

    psb1 = p.ps("psb1", [128, 1024], BF16)
    psb2 = p.ps("psb2", [128, 1024], BF16)
    psA = p.ps("psA", [128, 512], F32)
    psB = p.ps("psB", [128, 512], F32)
    psC = p.ps("psC", [128, 512], F32)
    psD = p.ps("psD", [128, 512], F32)
    psE = p.ps("psE", [128, 512], F32)
    psF = p.ps("psF", [128, 512], F32)
    sc_ps = (psA, psB, psC, psD)
    sc_ky = ("psA", "psB", "psC", "psD")

    p.dma("poolq", ident[:, :], identd, writes=("ident",))
    p.dma("spq", identf[:, :], identd, writes=("identf",))
    p.dma("poolq", ropeP[:, :], ropep_d, writes=("ropeP",))
    p.dma("spq", blamf[:, :], blam_d, writes=("blamf",))
    p.dma("poolq", bdenb[:, :], bden_d, writes=("bdenb",))
    p.dma("poolq", bsel8[:, :], bsel8_d, writes=("bsel8",))
    p.dma("poolq", dnm[:, :], dnm_d, writes=("dnm",))
    p.op("dve", lambda e: e.memset(onescb[:, :], 1.0), writes=("onescb",))
    p.op("dve", lambda e: e.memset(onesrf[:, :], 1.0), writes=("onesrf",))
    p.dma("spq", vecs[0:64, 0:1], bb_d, writes=("vecs",))
    p.dma("spq", vecs[0:64, 1:2], bm_d, writes=("vecs",))
    p.dma("spq", boutc[:, :], bo_d, writes=("boutc",))
    p.dma("spq", lngc[:, :], lng_d, writes=("lngc",))
    p.dma("spq", lnbc[:, :], lnb_d, writes=("lnbc",))
    p.dma("spq", gbc[:, :], gb_d, writes=("gbc",))
    p.dma("poolq", ebt[:, :], eb_d, writes=("ebt",))
    for kc in range(8):
        p.dma("poolq", gwt[:, kc * 8:(kc + 1) * 8],
              gw_d[kc * 128:(kc + 1) * 128, :], reads=("gwt",), writes=("gwt",))

    phiq = p.sb("phiq", [128, 8 * TC], BF16)
    phik = p.sb("phik", [128, 8 * TC], BF16)
    vt = p.sb("vt", [128, NTT * DM], BF16)

    ropec = p.sb("ropec", [128, 512], BF16)
    ropes = p.sb("ropes", [128, 512], BF16)
    xn = p.sb("xn", [128, 8 * TC], BF16)

    def rope_dma(t4):
        p.dma("poolq", ropec[:, :], ropec_d[:, t4 * 512:(t4 + 1) * 512],
              reads=("ropec",), writes=("ropec",))
        p.dma("poolq", ropes[:, :], ropes_d[:, t4 * 512:(t4 + 1) * 512],
              reads=("ropes",), writes=("ropes",))

    ln_names = emit_ln_x_to_xnfm(p, x_d, xn, psb2, ident, sqps=psA,
                                 sqps_key="psA")
    p.dma("spq", xnsp_d, xn[:, :], reads=("xn",), writes=("xnspill",))
    p.free(*ln_names)
    kwork = p.sb("kwork", [128, 512], F32)
    kb16 = p.sb("kb16", [128, 512], BF16)
    kswap = p.sb("kswap", [128, 512], F32)
    wbraw = p.sb("wbraw", [64, DM], BF16)

    emit_weight_T_tiles(p, wq_d, wraw, "wraw", w_tiles, "w_tiles",
                        ident, psb1, "psb1", 8, 8)
    emit_proj_rope_phi(p, w_tiles, "w_tiles", xn, psA, psD, ropeP, ropec,
                       ropes, kwork, kb16, kswap, phiq, "phiq",
                       rope_dma=rope_dma)
    emit_weight_T_tiles(p, wk_d, wraw, "wraw", w_tiles, "w_tiles", ident,
                        psb1, "psb1", 8, 8)
    emit_proj_rope_phi(p, w_tiles, "w_tiles", xn, psA, psD, ropeP, ropec,
                       ropes, kwork, kb16, kswap, phik, "phik",
                       rope_dma=rope_dma)
    emit_weight_T_tiles(p, wv_d, wraw, "wraw", w_tiles, "w_tiles", ident,
                        psb1, "psb1", 8, 8)
    for tt in range(NTT):
        for half in range(2):
            for j in range(4):
                pairs = [(xn[:, kc * TC + tt * 128: kc * TC + (tt + 1) * 128],
                          "xn",
                          w_tiles[:, (kc * 8 + half * 4 + j) * 128:
                                  (kc * 8 + half * 4 + j + 1) * 128],
                          "w_tiles") for kc in range(8)]
                emit_mm_acc(p, psC[:, j * 128:(j + 1) * 128], "psC", pairs)
            p.op("dve", lambda e, tt=tt, half=half: e.tensor_copy(
                vt[:, tt * DM + half * 512: tt * DM + (half + 1) * 512],
                psC[:, :]),
                reads=("psC",), writes=("vt",))
    for src_d, coff in ((wb_d, 0), (wm_d, 64)):
        p.dma("poolq", wbraw[:, :], src_d, reads=("wbraw",), writes=("wbraw",))
        for kc in range(8):
            p.op("pe", lambda e, kc=kc: e.transpose(
                psb1[:, 0:64], wbraw[0:64, kc * 128:(kc + 1) * 128],
                ident[0:64, 0:64]),
                reads=("wbraw", "ident"), writes=("psb1",))
            p.op("dve", lambda e, kc=kc, coff=coff: e.tensor_copy(
                wbm_tiles[:, kc * 128 + coff: kc * 128 + coff + 64],
                psb1[:, 0:64]),
                reads=("psb1",), writes=("wbm_tiles",))
    for t4 in range(NT4):
        for coff in (0, 64):
            pairs = [(wbm_tiles[:, kc * 128 + coff: kc * 128 + coff + 64],
                      "wbm_tiles",
                      xn[:, kc * TC + t4 * 512: kc * TC + (t4 + 1) * 512], "xn")
                     for kc in range(8)]
            emit_mm_acc(p, psB[0:64, :], "psB", pairs)
            if coff == 0:
                p.op("act", lambda e, t4=t4: e.activation(
                    beta[:, t4 * 512:(t4 + 1) * 512], psB[0:64, :], AF.Sigmoid,
                    bias=vecs[0:64, 0:1]),
                    reads=("psB", "vecs"), writes=("beta",))
                p.op("dve", lambda e, t4=t4: e.tensor_scalar(
                    beta[:, t4 * 512:(t4 + 1) * 512],
                    beta[:, t4 * 512:(t4 + 1) * 512],
                    0.85, 0.9995, OP.max, OP.min),
                    reads=("beta",), writes=("beta",))
            else:
                p.op("dve", lambda e, t4=t4: e.tensor_scalar(
                    mixl[:, t4 * 512:(t4 + 1) * 512], psB[0:64, :],
                    vecs[0:64, 1:2], None, OP.add),
                    reads=("psB", "vecs"), writes=("mixl",))
    p.free("wbraw", "kswap", "kb16", "kwork", "xn", "ropes", "ropec")

    mtm = p.sb("mtm", [128, 64], F32)
    mte = p.sb("mte", [128, 64], F32)
    mteb = p.sb("mteb", [128, 64], BF16)
    mcx = p.sb("mcx", [128, 32], F32)
    msm = p.sb("msm", [128, 32], F32)
    for tt in range(NTT):
        p.op("pe", lambda e, tt=tt: e.transpose(
            psb1[:, 0:64], mixl[0:64, tt * 128:(tt + 1) * 128],
            ident[0:64, 0:64]),
            reads=("mixl", "ident"), writes=("psb1",))
        p.op("dve", lambda e: e.tensor_copy(mtm[:, :], psb1[:, 0:64]),
             reads=("psb1",), writes=("mtm",))
        p.op("dve", lambda e: e.tensor_max(
            mcx[:, :], mtm[:, 0:64:2], mtm[:, 1:64:2]),
            reads=("mtm",), writes=("mcx",))
        p.op("dve", lambda e: e.tensor_sub(
            mtm[:, :].rearrange("p (h two) -> p h two", two=2),
            mtm[:, :].rearrange("p (h two) -> p h two", two=2),
            mcx[:, :].unsqueeze(2).broadcast_to([128, 32, 2])),
            reads=("mtm", "mcx"), writes=("mtm",))
        p.op("act", lambda e: e.activation(mte[:, :], mtm[:, :], AF.Exp),
             reads=("mtm",), writes=("mte",))
        p.op("dve", lambda e: e.tensor_add(
            msm[:, :], mte[:, 0:64:2], mte[:, 1:64:2]),
            reads=("mte",), writes=("msm",))
        p.op("dve", lambda e: e.reciprocal(msm[:, :], msm[:, :]),
             reads=("msm",), writes=("msm",))
        p.op("dve", lambda e: e.tensor_mul(
            mte[:, :].rearrange("p (h two) -> p h two", two=2),
            mte[:, :].rearrange("p (h two) -> p h two", two=2),
            msm[:, :].unsqueeze(2).broadcast_to([128, 32, 2])),
            reads=("mte", "msm"), writes=("mte",))
        p.op("dve", lambda e: e.tensor_copy(mteb[:, :], mte[:, :]),
             reads=("mte",), writes=("mteb",))
        p.op("pe", lambda e, tt=tt: e.transpose(
            psb1[0:64, 0:128], mteb[:, :], ident[:, :]),
            reads=("mteb", "ident"), writes=("psb1",))
        p.op("dve", lambda e, tt=tt: e.tensor_copy(
            mixl[0:64, tt * 128:(tt + 1) * 128], psb1[0:64, 0:128]),
            reads=("psb1",), writes=("mixl",))
    p.free("msm", "mcx", "mteb", "mte", "mtm")

    for c in range(NCH):
        sl = slice(c * CK, (c + 1) * CK)
        p.op("dve", lambda e, sl=sl: e.tensor_tensor_scan(
            lam[:, sl], beta[:, sl], beta[:, sl], 1.0, OP.mult, OP.bypass),
            reads=("beta",), writes=("lam",))
    p.op("dve", lambda e: e.tensor_copy(lamc[:, :], lam[:, CK - 1: TC: CK]),
         reads=("lam",), writes=("lamc",))
    for m in range(M):
        for ph in range(8):
            i = m * 8 + ph
            p.op("pe", lambda e, i=i: e.matmul(
                psE[:, 0:8], blamf[:, i * 128:(i + 1) * 128], lamc[:, :],
                start=True, stop=True),
                reads=("blamf", "lamc"), writes=("psE",))
            p.op("dve", lambda e, i=i: e.tensor_copy(
                lamcol[:, i * 8:(i + 1) * 8], psE[:, 0:8]),
                reads=("psE",), writes=("lamcol",))

    st8 = p.sb("st8", [128, 8 * 529], F32)
    ltm_sb = p.sb("ltm_sb", [1, 8], F32)
    lamc8 = p.sb("lamc8", [64, 8], F32)
    ltmb = p.sb("ltmb", [64, 8], F32)
    mcolb = p.sb("mcolb", [128, 8], F32)
    lamexp = p.sb("lamexp", [128, 16 * 8], F32)
    tmpS = p.sb("tmpS", [128, 32], F32)
    p.dma("spq", st8[:, :], st8_d, writes=("st8",))
    p.dma("spq", ltm_sb[:, :], ltm_d, writes=("ltm_sb",))
    p.op("dve", lambda e: e.tensor_copy(
        lamc8[:, :], st8[0:64, 528: 8 * 529: 529]),
        reads=("st8",), writes=("lamc8",))
    p.op("pe", lambda e: e.matmul(psE[0:64, 0:8], onesrf[0:1, 0:64],
                                  ltm_sb[:, :], start=True, stop=True),
         reads=("onesrf", "ltm_sb"), writes=("psE",))
    p.op("dve", lambda e: e.tensor_copy(ltmb[:, :], psE[0:64, 0:8]),
         reads=("psE",), writes=("ltmb",))
    p.op("pe", lambda e: e.matmul(psE[:, 0:8], onesrf[0:1, :], ltm_sb[:, :],
                                  start=True, stop=True),
         reads=("onesrf", "ltm_sb"), writes=("psE",))
    p.op("dve", lambda e: e.tensor_copy(mcolb[:, :], psE[:, 0:8]),
         reads=("psE",), writes=("mcolb",))
    p.op("dve", lambda e: e.tensor_mul(lamc8[:, :], lamc8[:, :], ltmb[:, :]),
         reads=("lamc8", "ltmb"), writes=("lamc8",))
    p.op("dve", lambda e: e.tensor_scalar(ltmb[:, :], ltmb[:, :], -1.0, 1.0,
                                          OP.mult, OP.add),
         reads=("ltmb",), writes=("ltmb",))
    p.op("dve", lambda e: e.tensor_add(lamc8[:, :], lamc8[:, :], ltmb[:, :]),
         reads=("lamc8", "ltmb"), writes=("lamc8",))
    for m in range(M):
        for ph in range(8):
            i = m * 8 + ph
            p.op("pe", lambda e, i=i: e.matmul(
                psE[:, 0:8], blamf[:, i * 128:(i + 1) * 128], lamc8[:, :],
                start=True, stop=True),
                reads=("blamf", "lamc8"), writes=("psE",))
            p.op("dve", lambda e, i=i: e.tensor_copy(
                lamexp[:, i * 8:(i + 1) * 8], psE[:, 0:8]),
                reads=("psE",), writes=("lamexp",))
    p.op("dve", lambda e: e.memset(S_sb[:, :], 0.0), writes=("S_sb",))
    p.op("dve", lambda e: e.memset(K_sb[:, :], 0.0), writes=("K_sb",))
    for j in range(7):
        for pc in range(16):
            p.op("dve", lambda e, j=j, pc=pc: e.tensor_scalar(
                tmpS[:, :], st8[:, j * 529 + pc * 32: j * 529 + (pc + 1) * 32],
                mcolb[:, j:j + 1], None, OP.mult),
                reads=("st8", "mcolb"), writes=("tmpS",))
            p.op("dve", lambda e, j=j, pc=pc: e.scalar_tensor_tensor(
                S_sb[:, pc * 32:(pc + 1) * 32], S_sb[:, pc * 32:(pc + 1) * 32],
                lamexp[:, pc * 8 + j: pc * 8 + j + 1], tmpS[:, :],
                OP.mult, OP.add),
                reads=("S_sb", "lamexp", "tmpS"), writes=("S_sb",))
            p.op("dve", lambda e, j=j, pc=pc: e.tensor_scalar(
                tmpS[:, 0:1], st8[:, j * 529 + 512 + pc: j * 529 + 513 + pc],
                mcolb[:, j:j + 1], None, OP.mult),
                reads=("st8", "mcolb"), writes=("tmpS",))
            p.op("dve", lambda e, j=j, pc=pc: e.scalar_tensor_tensor(
                K_sb[:, pc:pc + 1], K_sb[:, pc:pc + 1],
                lamexp[:, pc * 8 + j: pc * 8 + j + 1], tmpS[:, 0:1],
                OP.mult, OP.add),
                reads=("K_sb", "lamexp", "tmpS"), writes=("K_sb",))
    p.free("tmpS", "lamexp", "mcolb", "ltmb", "lamc8", "ltm_sb", "st8")

    bscr = p.sb("bscr", [128, TC], F32)
    Kfull = p.sb("Kfull", [128, TC], F32)
    qK = p.sb("qK", [128, TC], BF16)
    for m in range(M):
        for ph in range(8):
            i = m * 8 + ph
            for t4 in range(NT4):
                p.op("pe", lambda e, i=i, t4=t4: e.matmul(
                    psE[:, :], blamf[:, i * 128:(i + 1) * 128],
                    beta[:, t4 * 512:(t4 + 1) * 512], start=True, stop=True),
                    reads=("blamf", "beta"), writes=("psE",))
                p.op("dve", lambda e, t4=t4: e.tensor_copy(
                    bscr[:, t4 * 512:(t4 + 1) * 512], psE[:, :]),
                    reads=("psE",), writes=("bscr",))
            p.op("dve", lambda e, ph=ph, i=i: e.tensor_tensor_scan(
                Kfull[:, :], bscr[:, :],
                phik[:, ph * TC:(ph + 1) * TC], K_sb[:, i:i + 1],
                OP.mult, OP.add),
                reads=("bscr", "phik", "K_sb"), writes=("Kfull",))
            p.op("dve", lambda e, ph=ph: e.tensor_mul(
                qK[:, :], phiq[:, ph * TC:(ph + 1) * TC], Kfull[:, :]),
                reads=("phiq", "Kfull"), writes=("qK",))
            for t4 in range(NT4):
                psd, psk = sc_ps[t4], sc_ky[t4]
                p.op("pe", lambda e, i=i, t4=t4, psd=psd: e.matmul(
                    psd[0:64, :], bdenb[:, i * 64:(i + 1) * 64],
                    qK[:, t4 * 512:(t4 + 1) * 512],
                    start=(i == 0), stop=(i == 15)),
                    reads=("bdenb", "qK"), writes=(psk,))
    for t4 in range(NT4):
        psd, psk = sc_ps[t4], sc_ky[t4]
        p.op("dve", lambda e, t4=t4, psd=psd: e.tensor_scalar(
            sden[:, t4 * 512:(t4 + 1) * 512], psd[0:64, :], DN_EPS, None,
            OP.add),
            reads=(psk,), writes=("sden",))
    p.op("dve", lambda e: e.reciprocal(sden[:, :], sden[:, :]),
         reads=("sden",), writes=("sden",))
    p.op("dve", lambda e: e.tensor_mul(sden[:, :], sden[:, :], mixl[0:64, :]),
         reads=("sden", "mixl"), writes=("sden",))
    p.free("qK", "Kfull", "bscr")

    qsc = p.sb("qsc", [128, 8 * CK], BF16)
    ksc = p.sb("ksc", [128, 8 * CK], BF16)
    khat = p.sb("khat", [128, 8 * CK], BF16)
    khat_tm = p.sb("khat_tm", [128, 2 * DM], BF16)
    S16 = p.sb("S16", [128, 16 * 32], BF16)
    scM = p.sb("scM", [128, 4 * 384], BF16)
    sbc = p.sb("sbc", [128, 256], BF16)
    tmp0 = p.sb("tmp0", [128, 256], BF16)
    tmp1 = p.sb("tmp1", [128, 256], BF16)
    fac = p.sb("fac", [64, CK], F32)
    st_t = dict(phik=phik, vt=vt, lamc=lamc, blamf=blamf,
                rlam_ap=lambda c: lamr[:, :], rlam_key="lamr",
                fac=fac, khat=khat, khat_tm=khat_tm, ident=ident,
                ones_col_b=onescb, S_sb=S_sb, K_sb=K_sb,
                lamcol=lamcol, psb1=psb1, psE=psE, psF=psF)
    for c in range(NCH):
        csl = slice(c * CK, (c + 1) * CK)
        p.op("dve", lambda e, csl=csl: e.reciprocal(lamr[:, :], lam[:, csl]),
             reads=("lam",), writes=("lamr",))
        p.op("dve", lambda e: e.tensor_copy(S16[:, :], S_sb[:, :]),
             reads=("S_sb",), writes=("S16",))
        for ph in range(8):
            for m in range(M):
                i = m * 8 + ph
                p.op("pe", lambda e, i=i, csl=csl: e.matmul(
                    psE[:, 0:CK], blamf[:, i * 128:(i + 1) * 128],
                    lam[:, csl], start=True, stop=True),
                    reads=("blamf", "lam"), writes=("psE",))
                p.op("dve", lambda e, ph=ph, csl=csl: e.tensor_mul(
                    qsc[:, ph * CK:(ph + 1) * CK],
                    phiq[:, ph * TC + csl.start: ph * TC + csl.stop],
                    psE[:, 0:CK]),
                    reads=("phiq", "psE"), writes=("qsc",))
                p.op("pe", lambda e, i=i: e.matmul(
                    psE[:, 0:CK], blamf[:, i * 128:(i + 1) * 128],
                    lamr[:, :], start=True, stop=True),
                    reads=("blamf", "lamr"), writes=("psE",))
                p.op("dve", lambda e, ph=ph, csl=csl: e.tensor_mul(
                    ksc[:, ph * CK:(ph + 1) * CK],
                    phik[:, ph * TC + csl.start: ph * TC + csl.stop],
                    psE[:, 0:CK]),
                    reads=("phik", "psE"), writes=("ksc",))
                for j in range(4):
                    ps, pk = sc_ps[j], sc_ky[j]
                    p.op("pe", lambda e, j=j, ph=ph, ps=ps: e.matmul(
                        ps[:, 0:256],
                        ksc[32 * j:32 * (j + 1), ph * CK: ph * CK + 128],
                        qsc[32 * j:32 * (j + 1), ph * CK:(ph + 1) * CK],
                        start=True, stop=True, tile_position=(32 * j, 0)),
                        reads=("ksc", "qsc"), writes=(pk,))
                    p.op("pe", lambda e, j=j, ph=ph, ps=ps: e.matmul(
                        ps[:, 256:384],
                        ksc[32 * j:32 * (j + 1), ph * CK + 128:(ph + 1) * CK],
                        qsc[32 * j:32 * (j + 1), ph * CK + 128:(ph + 1) * CK],
                        start=True, stop=True, tile_position=(32 * j, 0)),
                        reads=("ksc", "qsc"), writes=(pk,))
                for j in range(4):
                    ps, pk = sc_ps[j], sc_ky[j]
                    p.op("dve", lambda e, j=j, ps=ps: e.tensor_mul(
                        scM[:, j * 384: j * 384 + 256], ps[:, 0:256],
                        dnm[:, 0:256]),
                        reads=(pk, "dnm"), writes=("scM",))
                    p.op("dve", lambda e, j=j, ps=ps: e.tensor_mul(
                        scM[:, j * 384 + 256: j * 384 + 384], ps[:, 256:384],
                        dnm[:, 0:128]),
                        reads=(pk, "dnm"), writes=("scM",))
                for j in range(4):
                    tt0 = c * 2
                    osl = psE[32 * j:32 * (j + 1), 0:256]
                    osl1 = psE[32 * j:32 * (j + 1), 128:256]
                    p.op("pe", lambda e, j=j, ph=ph, tt0=tt0, osl=osl: e.matmul(
                        osl,
                        vt[:, tt0 * DM + ph * 128 + j * 32:
                           tt0 * DM + ph * 128 + (j + 1) * 32],
                        scM[:, j * 384: j * 384 + 256],
                        start=True, stop=False, tile_position=(0, 32 * j)),
                        reads=("vt", "scM"), writes=("psE",))
                    p.op("pe", lambda e, j=j, ph=ph, tt0=tt0, osl1=osl1:
                         e.matmul(
                        osl1,
                        vt[:, (tt0 + 1) * DM + ph * 128 + j * 32:
                           (tt0 + 1) * DM + ph * 128 + (j + 1) * 32],
                        scM[:, j * 384 + 256: j * 384 + 384],
                        start=False, stop=False, tile_position=(0, 32 * j)),
                        reads=("vt", "scM"), writes=("psE",))
                    p.op("pe", lambda e, j=j, ph=ph, m=m, osl=osl: e.matmul(
                        osl,
                        S16[32 * j:32 * (j + 1),
                            (m * 8 + ph) * 32:(m * 8 + ph + 1) * 32],
                        qsc[32 * j:32 * (j + 1), ph * CK:(ph + 1) * CK],
                        start=False, stop=True, tile_position=(32 * j, 32 * j)),
                        reads=("S16", "qsc"), writes=("psE",))
                p.op("pe", lambda e, i=i, csl=csl: e.matmul(
                    psF[:, 0:CK], blamf[:, i * 128:(i + 1) * 128],
                    sden[:, csl], start=True, stop=True),
                    reads=("blamf", "sden"), writes=("psF",))
                p.op("dve", lambda e: e.tensor_copy(sbc[:, :], psF[:, 0:CK]),
                     reads=("psF",), writes=("sbc",))
                if m == 0:
                    p.op("dve", lambda e: e.tensor_mul(
                        tmp0[:, :], psE[:, 0:256], sbc[:, :]),
                        reads=("psE", "sbc"), writes=("tmp0",))
                else:
                    p.op("dve", lambda e: e.tensor_mul(
                        tmp1[:, :], psE[:, 0:256], sbc[:, :]),
                        reads=("psE", "sbc"), writes=("tmp1",))
                    p.op("dve", lambda e: e.tensor_add(
                        tmp0[:, :], tmp0[:, :], tmp1[:, :]),
                        reads=("tmp0", "tmp1"), writes=("tmp0",))
                    p.dma("spq",
                          ysp_d[:, ph * TC + csl.start: ph * TC + csl.stop],
                          tmp0[:, :], reads=("tmp0",), writes=("yspill",))
        emit_state_chunk(p, st_t, c)
    p.free("fac", "tmp1", "tmp0", "sbc", "scM", "S16", "khat_tm", "khat",
           "ksc", "qsc")
    p.free("vt", "phik", "phiq")

    emit_weight_T_tiles(p, wo_d, wraw, "wraw", w_tiles, "w_tiles", ident,
                        psb1, "psb1", 8, 8)
    xn2 = p.sb("xn2", [128, 8 * TC], BF16)
    s0 = p.sb("s0", [128, 8 * TC], BF16)
    ybuf = p.sb("ybuf", [128, 8 * 512], BF16)
    zrow = p.sb("zrow", [1, TC], F32)
    zrow2 = p.sb("zrow2", [1, TC], F32)
    ztile = p.sb("ztile", [128, 512], F32)
    zt8 = p.sb("zt8", [128, 8 * 512], F32)
    zb = p.sb("zb", [128, 512], BF16)
    zsq = p.sb("zsq", [128, 512], BF16)
    ysp_r = ysp_d.rearrange("p (kc w) -> p kc w", kc=8)
    zsp_r = zsp_d.rearrange("p (pc w) -> p pc w", pc=8)
    p.dma("spq", xn2[:, :], xnsp_d, reads=("xnspill",), writes=("xn2",))
    for t4 in range(NT4):
        p.dma("spq", ybuf[:, :].rearrange("p (kc w) -> p kc w", kc=8),
              ysp_r[:, :, t4 * 512:(t4 + 1) * 512],
              reads=("yspill", "ybuf"), writes=("ybuf",))
        for pc in range(8):
            zsl = slice(pc * 512, (pc + 1) * 512)
            pairs = [(w_tiles[:, (kc * 8 + pc) * 128:(kc * 8 + pc + 1) * 128],
                      "w_tiles", ybuf[:, kc * 512:(kc + 1) * 512], "ybuf")
                     for kc in range(8)]
            emit_mm_acc(p, psA[:, :], "psA", pairs)
            p.op("dve", lambda e, pc=pc, t4=t4, zsl=zsl: e.scalar_tensor_tensor(
                zt8[:, zsl], psA[:, :], boutc[:, pc:pc + 1],
                xn2[:, pc * TC + t4 * 512: pc * TC + (t4 + 1) * 512],
                OP.add, OP.add),
                reads=("psA", "boutc", "xn2"), writes=("zt8",))
            p.op("dve", lambda e, zsl=zsl: e.tensor_copy(zb[:, :], zt8[:, zsl]),
                 reads=("zt8",), writes=("zb",))
            p.op("act", lambda e, zsl=zsl: e.activation(zsq[:, :], zt8[:, zsl],
                                                        AF.Square),
                 reads=("zt8",), writes=("zsq",))
            p.op("pe", lambda e, pc=pc: e.matmul(
                psE[0:1, :], onescb[:, 0:1], zb[:, :],
                start=(pc == 0), stop=(pc == 7)),
                reads=("onescb", "zb"), writes=("psE",))
            p.op("pe", lambda e, pc=pc: e.matmul(
                psF[0:1, :], onescb[:, 0:1], zsq[:, :],
                start=(pc == 0), stop=(pc == 7)),
                reads=("onescb", "zsq"), writes=("psF",))
        p.dma("spq", zsp_r[:, :, t4 * 512:(t4 + 1) * 512],
              zt8[:, :].rearrange("p (pc w) -> p pc w", pc=8),
              reads=("zt8",), writes=("zspill",))
        tsl = slice(t4 * 512, (t4 + 1) * 512)
        p.op("dve", lambda e, tsl=tsl: e.tensor_scalar_mul(
            zrow[0:1, tsl], psE[0:1, :], 1.0 / DM),
            reads=("psE",), writes=("zrow",))
        p.op("dve", lambda e, tsl=tsl: e.tensor_scalar(
            zrow2[0:1, tsl], psF[0:1, :], 1.0 / DM, LN_EPS, OP.mult, OP.add),
            reads=("psF",), writes=("zrow2",))
        p.op("dve", lambda e, tsl=tsl: e.scalar_tensor_tensor(
            ztile[0:1, 0:512], zrow[0:1, tsl], -1.0, zrow[0:1, tsl],
            OP.mult, OP.mult),
            reads=("zrow",), writes=("ztile",))
        p.op("dve", lambda e, tsl=tsl: e.tensor_add(
            zrow2[0:1, tsl], zrow2[0:1, tsl], ztile[0:1, 0:512]),
            reads=("zrow2", "ztile"), writes=("zrow2",))
        p.op("act", lambda e, tsl=tsl: e.activation(
            zrow2[0:1, tsl], zrow2[0:1, tsl], AF.Sqrt),
            reads=("zrow2",), writes=("zrow2",))
        p.op("dve", lambda e, tsl=tsl: e.reciprocal(
            zrow2[0:1, tsl], zrow2[0:1, tsl]),
            reads=("zrow2",), writes=("zrow2",))
    for t4 in range(NT4):
        tsl = slice(t4 * 512, (t4 + 1) * 512)
        p.op("pe", lambda e, tsl=tsl: e.matmul(
            psE[:, :], onesrf[0:1, :], zrow[0:1, tsl], start=True, stop=True),
            reads=("onesrf", "zrow"), writes=("psE",))
        p.op("pe", lambda e, tsl=tsl: e.matmul(
            psF[:, :], onesrf[0:1, :], zrow2[0:1, tsl], start=True, stop=True),
            reads=("onesrf", "zrow2"), writes=("psF",))
        p.dma("spq", zt8[:, :].rearrange("p (pc w) -> p pc w", pc=8),
              zsp_r[:, :, t4 * 512:(t4 + 1) * 512],
              reads=("zspill", "zt8"), writes=("zt8",))
        for pc in range(8):
            sl = slice(pc * TC + t4 * 512, pc * TC + (t4 + 1) * 512)
            zsl = slice(pc * 512, (pc + 1) * 512)
            p.op("dve", lambda e, zsl=zsl: e.tensor_sub(
                zt8[:, zsl], zt8[:, zsl], psE[:, :]),
                reads=("zt8", "psE"), writes=("zt8",))
            p.op("dve", lambda e, zsl=zsl: e.tensor_mul(
                zt8[:, zsl], zt8[:, zsl], psF[:, :]),
                reads=("zt8", "psF"), writes=("zt8",))
            p.op("dve", lambda e, sl=sl, pc=pc, zsl=zsl: e.tensor_scalar(
                s0[:, sl], zt8[:, zsl], lngc[:, pc:pc + 1],
                lnbc[:, pc:pc + 1], OP.mult, OP.add),
                reads=("zt8", "lngc", "lnbc"), writes=("s0",))
    p.free("zsq", "zb", "zt8", "ztile", "zrow2", "zrow", "ybuf")

    glog = p.sb("glog", [8, TC], BF16)
    gtm = p.sb("gtm", [128, 8], F32)
    gtme = p.sb("gtme", [128, 8], F32)
    gtmb = p.sb("gtmb", [128, 8], BF16)
    gtmb_ap = [None]

    gmask = p.sb("gmask", [128, 8], F32)
    gcol = p.sb("gcol", [128, 4], F32)
    gtmb_ap[0] = gtmb[:, :]
    for t4 in range(NT4):
        pairs = [(gwt[:, kc * 8:(kc + 1) * 8], "gwt",
                  xn2[:, kc * TC + t4 * 512: kc * TC + (t4 + 1) * 512], "xn2")
                 for kc in range(8)]
        emit_mm_acc(p, psB[0:8, :], "psB", pairs)
        p.op("dve", lambda e, t4=t4: e.tensor_scalar(
            glog[:, t4 * 512:(t4 + 1) * 512], psB[0:8, :], gbc[0:8, 0:1],
            None, OP.add),
            reads=("psB", "gbc"), writes=("glog",))
    for tt in range(NTT):
        p.op("pe", lambda e, tt=tt: e.transpose(
            psb1[:, 0:8], glog[:, tt * 128:(tt + 1) * 128], ident[0:8, 0:8]),
            reads=("glog", "ident"), writes=("psb1",))
        p.op("dve", lambda e: e.tensor_copy(gtm[:, :], psb1[:, 0:8]),
             reads=("psb1",), writes=("gtm",))
        p.op("dve", lambda e: e.tensor_reduce(
            gcol[:, 0:1], gtm[:, :], AX.X, OP.max, negate=True),
            reads=("gtm",), writes=("gcol",))
        p.op("act", lambda e: e.activation(
            gtme[:, :], gtm[:, :], AF.Exp, bias=gcol[:, 0:1],
            accum_out=gcol[:, 1:2]),
            reads=("gtm", "gcol"), writes=("gtme", "gcol"))
        p.op("dve", lambda e: e.reciprocal(gcol[:, 1:2], gcol[:, 1:2]),
             reads=("gcol",), writes=("gcol",))
        p.op("dve", lambda e: e.tensor_scalar(
            gtme[:, :], gtme[:, :], gcol[:, 1:2], None, OP.mult),
            reads=("gtme", "gcol"), writes=("gtme",))
        p.op("dve", lambda e: e.tensor_single_scalar(
            gmask[:, :], gtme[:, :], 0.1, OP.is_ge),
            reads=("gtme",), writes=("gmask",))
        p.op("dve", lambda e: e.tensor_mul(gtme[:, :], gtme[:, :],
                                           gmask[:, :]),
             reads=("gtme", "gmask"), writes=("gtme",))
        p.op("dve", lambda e: e.reduce_sum(gcol[:, 3:4], gtme[:, :], AX.X),
             reads=("gtme",), writes=("gcol",))
        p.op("dve", lambda e: e.reciprocal(gcol[:, 3:4], gcol[:, 3:4]),
             reads=("gcol",), writes=("gcol",))
        p.op("dve", lambda e: e.tensor_scalar(
            gtme[:, :], gtme[:, :], gcol[:, 3:4], None, OP.mult),
            reads=("gtme", "gcol"), writes=("gtme",))
        p.op("dve", lambda e: e.tensor_copy(gmask[:, :], gtme[:, :]) if False
             else e.tensor_copy(gtmb_ap[0], gtme[:, :]),
            reads=("gtme",), writes=("gtmb",))
        p.op("pe", lambda e, tt=tt: e.transpose(
            psb1[0:8, 0:128], gtmb_ap[0], ident[:, :]),
            reads=("gtmb", "ident"), writes=("psb1",))
        p.op("dve", lambda e, tt=tt: e.tensor_copy(
            glog[:, tt * 128:(tt + 1) * 128], psb1[0:8, 0:128]),
            reads=("psb1",), writes=("glog",))
    p.free("gcol", "gmask", "gtmb", "gtme", "gtm")

    ymoe = p.sb("ymoe", [128, 8 * 1024], F32)
    xwt = p.sb("xwt", [128, 8 * 512], BF16)
    ewt = p.sb("ewt", [128, 64 * 128], BF16)
    sgt = p.sb("sgt", [128, 512], BF16)
    xld = p.sb("xld", [128, DM], F32)
    x1t8 = p.sb("x1t8", [128, 8 * 128], F32)
    x1_r = x1_d.rearrange("(pc p) w -> p pc w", p=128)
    for hh in range(2):
        t4s = (2 * hh, 2 * hh + 1)
        for e_c in range(8):
            p.dma("poolq", ewt[:, :],
                  ew_d[e_c * 128:(e_c + 1) * 128, :],
                  reads=("ewt",), writes=("ewt",))
            for t4 in t4s:
                th = t4 - 2 * hh
                p.op("pe", lambda e, e_c=e_c, t4=t4: e.matmul(
                    psE[:, :], bsel8[:, e_c * 128:(e_c + 1) * 128],
                    glog[:, t4 * 512:(t4 + 1) * 512],
                    start=True, stop=True),
                    reads=("bsel8", "glog"), writes=("psE",))
                for kc in range(8):
                    p.op("dve", lambda e, kc=kc, t4=t4: e.tensor_mul(
                        xwt[:, kc * 512:(kc + 1) * 512],
                        xn2[:, kc * TC + t4 * 512: kc * TC + (t4 + 1) * 512],
                        psE[:, :]),
                        reads=("xn2", "psE"), writes=("xwt",))
                for pc in range(8):
                    pairs = [(ewt[:, (kc * 8 + pc) * 128:
                                  (kc * 8 + pc + 1) * 128], "ewt",
                              xwt[:, kc * 512:(kc + 1) * 512], "xwt")
                             for kc in range(8)]
                    emit_mm_acc(p, psA[:, :], "psA", pairs)
                    sl = slice(pc * 1024 + th * 512, pc * 1024 + (th + 1) * 512)
                    if e_c == 0:
                        p.op("dve", lambda e, sl=sl: e.tensor_copy(
                            ymoe[:, sl], psA[:, :]),
                            reads=("psA",), writes=("ymoe",))
                    else:
                        p.op("dve", lambda e, sl=sl: e.tensor_add(
                            ymoe[:, sl], ymoe[:, sl], psA[:, :]),
                            reads=("psA", "ymoe"), writes=("ymoe",))
        for pc in range(8):
            for t4 in t4s:
                th = t4 - 2 * hh
                sl = slice(pc * 1024 + th * 512, pc * 1024 + (th + 1) * 512)
                p.op("pe", lambda e, pc=pc, t4=t4: e.matmul(
                    psA[:, :], ebt[:, pc * 128:(pc + 1) * 128],
                    glog[:, t4 * 512:(t4 + 1) * 512],
                    start=True, stop=True),
                    reads=("ebt", "glog"), writes=("psA",))
                p.op("dve", lambda e, sl=sl: e.tensor_add(
                    ymoe[:, sl], ymoe[:, sl], psA[:, :]),
                    reads=("psA", "ymoe"), writes=("ymoe",))
                p.op("act", lambda e, sl=sl: e.activation(
                    sgt[:, :], ymoe[:, sl], AF.Silu),
                    reads=("ymoe",), writes=("sgt",))
                sl2 = slice(pc * TC + t4 * 512, pc * TC + (t4 + 1) * 512)
                p.op("dve", lambda e, sl=sl, sl2=sl2: e.tensor_mul(
                    ymoe[:, sl], s0[:, sl2], sgt[:, :]),
                    reads=("s0", "sgt", "ymoe"), writes=("ymoe",))
        for tt in range(8 * hh, 8 * (hh + 1)):
            p.dma("spq", xld[:, :], x_d[tt * 128:(tt + 1) * 128, :],
                  reads=("xld",), writes=("xld",))
            for pc in range(8):
                p.op("pe", lambda e, pc=pc: e.transpose(
                    psA[:, 0:128], xld[:, pc * 128:(pc + 1) * 128],
                    identf[:, :]),
                    reads=("xld", "identf"), writes=("psA",))
                sl = slice(pc * 1024 + (tt - 8 * hh) * 128,
                           pc * 1024 + (tt - 8 * hh + 1) * 128)
                p.op("dve", lambda e, sl=sl, pc=pc: e.tensor_add(
                    x1t8[:, pc * 128:(pc + 1) * 128], psA[:, 0:128],
                    ymoe[:, sl]),
                    reads=("psA", "ymoe"), writes=("x1t8",))
            p.dma("spq", x1_r[:, :, tt * 128:(tt + 1) * 128],
                  x1t8[:, :].rearrange("p (pc w) -> p pc w", pc=8),
                  reads=("x1t8",), writes=("x1_d",))

    if debug:
        p.dump("beta", beta[:, :], ("beta",))
        p.dump("mixl", mixl[:, :], ("mixl",))
        p.dump("sden", sden[:, :], ("sden",))
        p.dump("s0", s0[:, :], ("s0",))
        p.dump("Ssb", S_sb[:, :], ("S_sb",))
        p.dump("Ksb", K_sb[:, :], ("K_sb",))
        p.dump("glog", glog[:, :], ("glog",))

    p.finalize()
    return nc


def sc_t4_slice(t4):
    return slice(t4 * 512, (t4 + 1) * 512)


_L0B_CACHE = [None]


def _bsel8_host():
    b = np.zeros((8, 8 * 128), np.float32)
    for e in range(8):
        b[e, e * 128:(e + 1) * 128] = 1.0
    return b


def run_l0b(inputs, states):
    x = np.ascontiguousarray(np.asarray(inputs["L0_attn_x"]).reshape(TFULL, DM))
    if _L0B_CACHE[0] is None:
        _L0B_CACHE[0] = build_l0b(debug=bool(int(
            os.environ.get("KDEBUG", "0"))))
    nc = _L0B_CACHE[0]
    st8 = np.ascontiguousarray(np.asarray(states, np.float32
                                          ).transpose(1, 0, 2).reshape(128, 8 * 529))
    def f32(k):
        return np.ascontiguousarray(np.asarray(inputs[k], np.float32))
    in_maps = []
    for c in range(NCORES):
        tl = host_tables(c)
        in_maps.append({
            "x": np.ascontiguousarray(x[c * TC:(c + 1) * TC]),
            "Wq": f32("L0_attn_Wq"), "Wk": f32("L0_attn_Wk"),
            "Wv": f32("L0_attn_Wv"),
            "Wbeta": f32("L0_attn_Wbeta"),
            "bbeta": f32("L0_attn_bbeta").reshape(64, 1),
            "Wmix": f32("L0_attn_Wmix"),
            "bmix": f32("L0_attn_bmix").reshape(64, 1),
            "Wout": f32("L0_attn_Wout"),
            "bout": np.ascontiguousarray(
                f32("L0_attn_bout").reshape(8, 128).T),
            "ln_g": np.ascontiguousarray(
                f32("L0_attn_ln_g").reshape(8, 128).T),
            "ln_b": np.ascontiguousarray(
                f32("L0_attn_ln_b").reshape(8, 128).T),
            "gate_w": f32("L0_ffn_gate_w"),
            "gate_b": f32("L0_ffn_gate_b").reshape(8, 1),
            "expert_w": np.ascontiguousarray(
                f32("L0_ffn_expert_w").reshape(8, 8, 128, DM)
                .transpose(0, 2, 1, 3).reshape(8 * 128, 64 * 128)),
            "expert_b": f32("L0_ffn_expert_b"),
            "states8": st8,
            "ltmask": tl["ltmask"],
            "ident": tl["ident"], "ropec": tl["ropec"], "ropes": tl["ropes"],
            "ropeP": tl["ropeP"], "Blam": tl["Blam"], "Bden": tl["Bden"],
            "bsel8": _bsel8_host(), "dn_mask": tl["dn_mask"],
        })
    res = run_bass_kernel_spmd(nc, in_maps, core_ids=list(range(NCORES)))
    run_l0b.last = res
    return np.stack([res.results[c]["x1"] for c in range(NCORES)])



LEXT = 1152
WEXT = LEXT + TC + 384
IEXT = 1024
NEG = -30000.0


def emit_ln_rows(p, x_dram, width, murow, rsrow, xtile, zb, zsq, psE, psF,
                 onescb, pref, rws=None, xt8=None, xt8key=None):
    nt = width // 512
    xr = (x_dram.rearrange("(pc p) w -> p pc w", p=128)
          if xt8 is not None else None)
    for t4 in range(nt):
        if xt8 is not None:
            p.dma("spq", xt8[:, :].rearrange("p (pc w) -> p pc w", pc=8),
                  xr[:, :, t4 * 512:(t4 + 1) * 512],
                  reads=(xt8key,), writes=(xt8key,))
        for pc in range(8):
            if xt8 is not None:
                src, skey = xt8[:, pc * 512:(pc + 1) * 512], xt8key
            else:
                p.dma("spq", xtile[:, :],
                      x_dram[pc * 128:(pc + 1) * 128,
                             t4 * 512:(t4 + 1) * 512],
                      reads=(pref + "xtile",), writes=(pref + "xtile",))
                src, skey = xtile[:, :], pref + "xtile"
            p.op("dve", lambda e, src=src: e.tensor_copy(zb[:, :], src),
                 reads=(skey,), writes=(pref + "zb",))
            p.op("act", lambda e, src=src: e.activation(zsq[:, :], src,
                                                        AF.Square),
                 reads=(skey,), writes=(pref + "zsq",))
            p.op("pe", lambda e, pc=pc: e.matmul(
                psE[0:1, :], onescb[:, 0:1], zb[:, :],
                start=(pc == 0), stop=(pc == 7)),
                reads=("onescb", pref + "zb"), writes=("psE",))
            p.op("pe", lambda e, pc=pc: e.matmul(
                psF[0:1, :], onescb[:, 0:1], zsq[:, :],
                start=(pc == 0), stop=(pc == 7)),
                reads=("onescb", pref + "zsq"), writes=("psF",))
        tsl = slice(t4 * 512, (t4 + 1) * 512)
        r1 = rws[0:1, 0:512]
        r2 = rws[0:1, 512:1024]
        p.op("dve", lambda e, tsl=tsl: e.tensor_scalar_mul(
            murow[0:1, tsl], psE[0:1, :], 1.0 / DM),
            reads=("psE",), writes=(pref + "mu",))
        p.op("dve", lambda e, r1=r1: e.tensor_scalar(
            r1, psF[0:1, :], 1.0 / DM, LN_EPS, OP.mult, OP.add),
            reads=("psF",), writes=(pref + "rw",))
        p.op("dve", lambda e, tsl=tsl, r2=r2: e.scalar_tensor_tensor(
            r2, murow[0:1, tsl], -1.0, murow[0:1, tsl],
            OP.mult, OP.mult),
            reads=(pref + "mu",), writes=(pref + "rw",))
        p.op("dve", lambda e, r1=r1, r2=r2: e.tensor_add(r1, r1, r2),
             reads=(pref + "rw",), writes=(pref + "rw",))
        p.op("act", lambda e, r1=r1: e.activation(r1, r1, AF.Sqrt),
             reads=(pref + "rw",), writes=(pref + "rw",))
        p.op("dve", lambda e, r1=r1: e.reciprocal(r1, r1),
             reads=(pref + "rw",), writes=(pref + "rw",))
        p.op("dve", lambda e, tsl=tsl, r1=r1: e.tensor_copy(
            rsrow[0:1, tsl], r1),
            reads=(pref + "rw",), writes=(pref + "rs",))


class LnSrc:

    def __init__(self, p, x_dram, murow, rsrow, onesrb, psMu, psRs, xtile,
                 pref, kMu="psMu", kRs="psRs"):
        self.p, self.x = p, x_dram
        self.mu, self.rs, self.ones = murow, rsrow, onesrb
        self.psMu, self.psRs = psMu, psRs
        self.kMu, self.kRs = kMu, kRs
        self.xt = xtile
        self.pref = pref
        self.cur = None

    def bcast(self, col0, n):
        p, pref = self.p, self.pref
        if self.mu is None:
            return
        if self.cur == (col0, n):
            return
        self.cur = (col0, n)
        p.op("pe", lambda e, n=n, col0=col0: e.matmul(
            self.psMu[:, 0:n], self.ones[0:1, :], self.mu[0:1, col0:col0 + n],
            start=True, stop=True),
            reads=("onesrb", pref + "mu"), writes=(self.kMu,))
        p.op("pe", lambda e, n=n, col0=col0: e.matmul(
            self.psRs[:, 0:n], self.ones[0:1, :], self.rs[0:1, col0:col0 + n],
            start=True, stop=True),
            reads=("onesrb", pref + "rs"), writes=(self.kRs,))

    def group_load(self, col0, n, xt8, xt8key, shift4=False, q="spq"):
        p = self.p
        xr = self.x.rearrange("(pc p) w -> p pc w", p=128)
        if shift4:
            p.dma(q, xt8[:, 0:4 * n].rearrange("p (pc w) -> p pc w", pc=4),
                  xr[:, 0:4, col0:col0 + n],
                  reads=(xt8key,), writes=(xt8key,))
            p.dma(q, xt8[:, 4 * n:8 * n].rearrange("p (pc w) -> p pc w", pc=4),
                  xr[:, 4:8, col0 - 1:col0 + n - 1],
                  reads=(xt8key,), writes=(xt8key,))
        else:
            p.dma(q, xt8[:, 0:8 * n].rearrange("p (pc w) -> p pc w", pc=8),
                  xr[:, :, col0:col0 + n],
                  reads=(xt8key,), writes=(xt8key,))

    def norm_pc(self, pc, col0, n, xt8, xt8key, out_ap, out_key):
        p = self.p
        self.bcast(col0, n)
        sl = slice(pc * n, (pc + 1) * n)
        p.op("dve", lambda e, sl=sl, n=n: e.tensor_sub(
            xt8[:, sl], xt8[:, sl], self.psMu[:, 0:n]),
            reads=(xt8key, self.kMu), writes=(xt8key,))
        p.op("dve", lambda e, sl=sl, n=n, out_ap=out_ap: e.tensor_mul(
            out_ap, xt8[:, sl], self.psRs[:, 0:n]),
            reads=(xt8key, self.kRs), writes=(out_key,))

    def tile(self, pc, col0, n, out_ap, out_key, f32out=None, f32key=None):
        p, pref = self.p, self.pref
        if self.mu is None:
            p.dma("poolq", out_ap, self.x[pc * 128:(pc + 1) * 128,
                                          col0:col0 + n],
                  reads=(out_key,), writes=(out_key,))
            assert f32out is None
            return
        self.bcast(col0, n)
        p.dma("spq", self.xt[:, 0:n],
              self.x[pc * 128:(pc + 1) * 128, col0:col0 + n],
              reads=(pref + "xtile",), writes=(pref + "xtile",))
        p.op("dve", lambda e, n=n: e.tensor_sub(
            self.xt[:, 0:n], self.xt[:, 0:n], self.psMu[:, 0:n]),
            reads=(pref + "xtile", self.kMu), writes=(pref + "xtile",))
        if f32out is not None:
            p.op("dve", lambda e, n=n: e.tensor_mul(
                f32out, self.xt[:, 0:n], self.psRs[:, 0:n]),
                reads=(pref + "xtile", self.kRs), writes=(f32key,))
            p.op("dve", lambda e: e.tensor_copy(out_ap, f32out),
                 reads=(f32key,), writes=(out_key,))
        else:
            p.op("dve", lambda e, n=n: e.tensor_mul(
                out_ap, self.xt[:, 0:n], self.psRs[:, 0:n]),
                reads=(pref + "xtile", self.kRs), writes=(out_key,))


def emit_inner_attn(p, t, src, wqkv_d, qs_col, ks_col, xc_d, o_xdst):
    psA, psB, psC, psD = t["psA"], t["psB"], t["psC"], t["psD"]
    psE, psb1 = t["psE"], t["psb1"]
    ident, onesrb = t["ident"], t["onesrb"]
    hblk, wqt = t["hblk"], t["wqt"]
    qI, kI, vI = t["qI"], t["kI"], t["vI"]
    sqb, rnr, rnb = t["sqb"], t["rnr"], t["rnb"]
    bh8, bh8T = t["bh8"], t["bh8T"]
    xtb = t["xtb"]
    mxc, prbN, prbT = t["mxc"], t["prbN"], t["prbT"]
    TB1, TB2, w0r = t["TB1"], t["TB2"], t["w0r"]
    wo_t = t["wo_t"]
    o_fm = t["o_fm"]
    own0 = LEXT

    p.dma("poolq", wqt[:, :], wqkv_d[:, :], reads=("wqt",), writes=("wqt",))
    for et4 in range(6):
        c0 = own0 - IEXT + et4 * 512
        if src.mu is None:
            src.group_load(c0, 512, hblk, "hblk", shift4=True, q="poolq")
        else:
            xt8 = t["xt8"]
            src.group_load(c0, 512, xt8, "xt8", shift4=True)
            for kc in range(8):
                off = c0 - (1 if kc >= 4 else 0)
                src.norm_pc(kc, off, 512, xt8, "xt8",
                            hblk[:, kc * 512:(kc + 1) * 512], "hblk")
        for nc_ in range(8):
            pairs = [(wqt[:, (nc_ * 8 + kc) * 128:(nc_ * 8 + kc + 1) * 128],
                      "wqt", hblk[:, kc * 512:(kc + 1) * 512], "hblk")
                     for kc in range(8)]
            emit_mm_acc(p, psA[:, :], "psA", pairs)
            dst, dkey = (qI, "qI") if nc_ < 4 else (kI, "kI")
            pc = nc_ % 4
            p.op("dve", lambda e, dst=dst, pc=pc, et4=et4: e.tensor_copy(
                dst[:, pc * 3072 + et4 * 512: pc * 3072 + (et4 + 1) * 512],
                psA[:, :]),
                reads=("psA",), writes=(dkey,))
        for tc in range(4):
            for nc_ in range(8, 12):
                j = nc_ - 8
                pairs = [(hblk[:, kc * 512 + tc * 128: kc * 512 + (tc + 1) * 128],
                          "hblk",
                          wqt[:, (nc_ * 8 + kc) * 128:(nc_ * 8 + kc + 1) * 128],
                          "wqt") for kc in range(8)]
                emit_mm_acc(p, psA[:, j * 128:(j + 1) * 128], "psA", pairs)
            p.op("dve", lambda e, et4=et4, tc=tc: e.tensor_copy(
                vI[:, (et4 * 4 + tc) * 512:(et4 * 4 + tc + 1) * 512],
                psA[:, :]),
                reads=("psA",), writes=("vI",))
    for (buf, bkey, scol, skey, ci, si, scale8) in (
            (qI, "qI", qs_col, "qs_col", 0, 1, True),
            (kI, "kI", ks_col, "ks_col", 2, 3, False)):
        for et4 in range(6):
            for pc in range(4):
                sl = slice(pc * 3072 + et4 * 512, pc * 3072 + (et4 + 1) * 512)
                p.op("act", lambda e, sl=sl, buf=buf: e.activation(
                    sqb[:, :], buf[:, sl], AF.Square),
                    reads=(bkey,), writes=("sqb",))
                p.op("pe", lambda e, pc=pc: e.matmul(
                    psE[0:8, :], bh8[:, pc * 8:(pc + 1) * 8], sqb[:, :],
                    start=(pc == 0), stop=(pc == 3)),
                    reads=("bh8", "sqb"), writes=("psE",))
            tsl = slice(et4 * 512, (et4 + 1) * 512)
            p.op("act", lambda e, tsl=tsl: e.activation(
                rnr[0:8, tsl], psE[0:8, :], AF.Sqrt),
                reads=("psE",), writes=("rnr",))
            p.op("dve", lambda e, tsl=tsl: e.tensor_scalar_max(
                rnr[0:8, tsl], rnr[0:8, tsl], 1e-12),
                reads=("rnr",), writes=("rnr",))
            p.op("dve", lambda e, tsl=tsl: e.reciprocal(
                rnr[0:8, tsl], rnr[0:8, tsl]),
                reads=("rnr",), writes=("rnr",))
            if scale8:
                p.op("dve", lambda e, tsl=tsl: e.tensor_scalar_mul(
                    rnr[0:8, tsl], rnr[0:8, tsl], 8.0),
                    reads=("rnr",), writes=("rnr",))
            p.op("dve", lambda e, tsl=tsl: e.tensor_copy(
                rnb[0:8, tsl], rnr[0:8, tsl]),
                reads=("rnr",), writes=("rnb",))
        for et4 in range(6):
            c0 = own0 - IEXT + et4 * 512
            p.dma("poolq", xtb[:, 0:512], xc_d[ci][:, c0:c0 + 512],
                  reads=("xtb",), writes=("xtb",))
            p.dma("poolq", xtb[:, 512:1024], xc_d[si][:, c0:c0 + 512],
                  reads=("xtb",), writes=("xtb",))
            tsl = slice(et4 * 512, (et4 + 1) * 512)
            for pc in range(4):
                sl = slice(pc * 3072 + et4 * 512, pc * 3072 + (et4 + 1) * 512)
                p.op("pe", lambda e, pc=pc, tsl=tsl: e.matmul(
                    psE[:, :], bh8T[:, pc * 128:(pc + 1) * 128],
                    rnb[0:8, tsl], start=True, stop=True),
                    reads=("bh8T", "rnb"), writes=("psE",))
                p.op("dve", lambda e, sl=sl, scol=scol, buf=buf:
                     e.scalar_tensor_tensor(
                    buf[:, sl], buf[:, sl], scol, psE[:, :],
                    OP.mult, OP.mult),
                    reads=(bkey, skey, "psE"), writes=(bkey,))
                p.op("pe", lambda e, sl=sl, buf=buf: e.matmul(
                    psE[:, :], t["P32"][:, :], buf[:, sl],
                    start=True, stop=True),
                    reads=("P32", bkey), writes=("psE",))
                p.op("dve", lambda e, sl=sl, buf=buf: e.tensor_mul(
                    buf[:, sl], buf[:, sl], xtb[:, 0:512]),
                    reads=(bkey, "xtb"), writes=(bkey,))
                p.op("dve", lambda e: e.tensor_mul(
                    sqb[:, :], psE[:, :], xtb[:, 512:1024]),
                    reads=("psE", "xtb"), writes=("sqb",))
                p.op("dve", lambda e, sl=sl, buf=buf: e.tensor_add(
                    buf[:, sl], buf[:, sl], sqb[:, :]),
                    reads=(bkey, "sqb"), writes=(bkey,))
    units = [(-1, -128)]
    for wi in range(4):
        for qc in range(4):
            units.append((wi, 512 * wi + 128 * qc))
    for uii, (wi, q0) in enumerate(units):
        for hp in range(4):
            par = (uii * 4 + hp) % 2
            psPV = (psE, t["psF"])[par]
            kPV = ("psE", "psF")[par]
            prN = prbN[:, par * 640:(par + 1) * 640]
            prT = prbT[:, par * 640:(par + 1) * 640]
            kprN, kprT = "iprbN%d" % par, "iprbT%d" % par
            for h2 in range(2):
                psLo = (psA, psB)[h2]
                kLo = ("psA", "psB")[h2]
                psHi = (psC, psD)[h2]
                kHi = ("psC", "psD")[h2]
                mxv = mxc[:, 4 * par: 4 * par + 4]
                kmx = "mxc%d" % par
                h = 2 * hp + h2
                qsl = slice(hp * 3072 + q0 + IEXT, hp * 3072 + q0 + IEXT + 128)
                k0 = q0 - 512 + IEXT
                p.op("pe", lambda e, h2=h2, hp=hp, qsl=qsl, k0=k0, psLo=psLo:
                     e.matmul(psLo[:, :],
                              qI[64 * h2:64 * (h2 + 1), qsl],
                              kI[64 * h2:64 * (h2 + 1),
                                 hp * 3072 + k0: hp * 3072 + k0 + 512],
                              start=True, stop=False,
                              tile_position=(64 * h2, 0)),
                     reads=("qI", "kI"), writes=(kLo,))
                if wi == 0:
                    n0 = 512 - q0
                    p.op("pe", lambda e, n0=n0, psLo=psLo: e.matmul(
                        psLo[:, 0:n0], onesrb[0:1, :], w0r[0:1, 0:n0],
                        start=False, stop=False),
                        reads=("onesrb", "w0r"), writes=(kLo,))
                p.op("pe", lambda e, psLo=psLo: e.matmul(
                    psLo[:, 0:128], ident[:, :], TB1[:, :],
                    start=False, stop=True),
                    reads=("ident", "TB1"), writes=(kLo,))
                p.op("pe", lambda e, h2=h2, hp=hp, qsl=qsl, k0=k0, psHi=psHi:
                     e.matmul(psHi[:, 0:128],
                              qI[64 * h2:64 * (h2 + 1), qsl],
                              kI[64 * h2:64 * (h2 + 1),
                                 hp * 3072 + k0 + 512: hp * 3072 + k0 + 640],
                              start=True, stop=False,
                              tile_position=(64 * h2, 0)),
                     reads=("qI", "kI"), writes=(kHi,))
                p.op("pe", lambda e, psHi=psHi: e.matmul(
                    psHi[:, 0:128], ident[:, :], TB2[:, :],
                    start=False, stop=True),
                    reads=("ident", "TB2"), writes=(kHi,))
                p.op("dve", lambda e, psLo=psLo, mxv=mxv: e.tensor_reduce(
                    mxv[:, 0:1], psLo[:, :], AX.X, OP.max, negate=True),
                    reads=(kLo,), writes=(kmx,))
                p.op("dve", lambda e, psHi=psHi, mxv=mxv: e.tensor_reduce(
                    mxv[:, 1:2], psHi[:, 0:128], AX.X, OP.max, negate=True),
                    reads=(kHi,), writes=(kmx,))
                p.op("dve", lambda e, mxv=mxv: e.tensor_tensor(
                    mxv[:, 0:1], mxv[:, 0:1], mxv[:, 1:2], OP.min),
                    reads=(kmx,), writes=(kmx,))
                p.op("act", lambda e, psLo=psLo, prN=prN, mxv=mxv:
                     e.activation(
                    prN[:, 0:512], psLo[:, :], AF.Exp, bias=mxv[:, 0:1],
                    accum_out=mxv[:, 2:3]),
                    reads=(kLo, kmx), writes=(kprN, kmx))
                p.op("act", lambda e, psHi=psHi, prN=prN, mxv=mxv:
                     e.activation(
                    prN[:, 512:640], psHi[:, 0:128], AF.Exp, bias=mxv[:, 0:1],
                    accum_out=mxv[:, 3:4]),
                    reads=(kHi, kmx), writes=(kprN, kmx))
                p.op("dve", lambda e, mxv=mxv: e.tensor_add(
                    mxv[:, 2:3], mxv[:, 2:3], mxv[:, 3:4]),
                    reads=(kmx,), writes=(kmx,))
                p.op("dve", lambda e, mxv=mxv: e.reciprocal(
                    mxv[:, 2:3], mxv[:, 2:3]),
                    reads=(kmx,), writes=(kmx,))
                p.op("dve", lambda e, prN=prN, mxv=mxv: e.tensor_scalar(
                    prN[:, 0:640], prN[:, 0:640], mxv[:, 2:3], None,
                    OP.mult),
                    reads=(kprN, kmx), writes=(kprN,))
                for kb in range(5):
                    p.op("pe", lambda e, kb=kb, prN=prN: e.transpose(
                        psb1[:, kb * 128:(kb + 1) * 128],
                        prN[:, kb * 128:(kb + 1) * 128], ident[:, :]),
                        reads=(kprN, "ident"), writes=("psb1",))
                p.op("dve", lambda e, prT=prT: e.tensor_copy(
                    prT[:, 0:640], psb1[:, 0:640]),
                    reads=("psb1",), writes=(kprT,))
                for kb in range(5):
                    tcb = (q0 - 512 + IEXT) // 128 + kb
                    p.op("pe", lambda e, kb=kb, tcb=tcb, h=h, h2=h2,
                         psPV=psPV, prT=prT: e.matmul(
                        psPV[64 * h2:64 * (h2 + 1), 0:128],
                        vI[:, tcb * 512 + 64 * h: tcb * 512 + 64 * h + 64],
                        prT[:, kb * 128:(kb + 1) * 128],
                        start=(kb == 0), stop=(kb == 4),
                        tile_position=(0, 64 * h2)),
                        reads=("vI", kprT), writes=(kPV,))
            p.op("dve", lambda e, hp=hp, q0=q0, psPV=psPV: e.tensor_copy(
                o_fm[:, hp * 2176 + q0 + 128: hp * 2176 + q0 + 256],
                psPV[:, 0:128]),
                reads=(kPV,), writes=("o_fm",))
    for et in range(17):
        for pcout in range(8):
            pairs = [(wo_t[:, (pcout * 4 + kc) * 128:(pcout * 4 + kc + 1) * 128],
                      "wo_t", o_fm[:, kc * 2176 + et * 128:
                                   kc * 2176 + (et + 1) * 128], "o_fm")
                     for kc in range(4)]
            emit_mm_acc(p, psA[:, 0:128], "psA", pairs)
            o_xdst(et, pcout, psA[:, 0:128], "psA")


def emit_inner_ffn(p, t, xi_d, w1_d, b1_d, w2_d, b2c, lngc, lnbc, out_tile):
    psA, psB, psE, psF = t["psA"], t["psB"], t["psE"], t["psF"]
    onescb, onesrb = t["onescb"], t["onesrb"]
    mu2, rs2 = t["mu2"], t["rs2"]
    lnzb, lnzq = t["lnzb"], t["lnzq"]
    xnsh = t["xnsh"]
    wff, wk2 = t["wff"], t["wk2"]
    agb = t["agb"]
    b1c = t["b1c"]
    h2p_d = t["h2p_d"]
    pt8 = t["ptile8"]
    xt8 = t["xt8"]
    xi_r = xi_d.rearrange("(pc p) w -> p pc w", p=128)
    h2p_r = h2p_d.rearrange("(pc p) w -> p pc w", p=128)

    OW0 = 128
    for t4 in range(4):
        c0 = OW0 + t4 * 512
        p.dma("spq", xt8[:, 0:4 * 512].rearrange("p (pc w) -> p pc w", pc=4),
              xi_r[:, 0:4, c0:c0 + 512], reads=("xt8",), writes=("xt8",))
        p.dma("spq", xt8[:, 4 * 512:].rearrange("p (pc w) -> p pc w", pc=4),
              xi_r[:, 4:8, c0 - 1:c0 + 511], reads=("xt8",), writes=("xt8",))
        for pc in range(8):
            sl = slice(pc * 512, (pc + 1) * 512)
            p.op("dve", lambda e, sl=sl: e.tensor_copy(lnzb[:, :], xt8[:, sl]),
                 reads=("xt8",), writes=("f_zb",))
            p.op("act", lambda e, sl=sl: e.activation(lnzq[:, :], xt8[:, sl],
                                                      AF.Square),
                 reads=("xt8",), writes=("f_zq",))
            p.op("pe", lambda e, pc=pc: e.matmul(
                psE[0:1, :], onescb[:, 0:1], lnzb[:, :],
                start=(pc == 0), stop=(pc == 7)),
                reads=("onescb", "f_zb"), writes=("psE",))
            p.op("pe", lambda e, pc=pc: e.matmul(
                psF[0:1, :], onescb[:, 0:1], lnzq[:, :],
                start=(pc == 0), stop=(pc == 7)),
                reads=("onescb", "f_zq"), writes=("psF",))
        tsl = slice(t4 * 512, (t4 + 1) * 512)
        r1 = t["rws"][0:1, 0:512]
        r2 = t["rws"][0:1, 512:1024]
        p.op("dve", lambda e, tsl=tsl: e.tensor_scalar_mul(
            mu2[0:1, tsl], psE[0:1, :], 1.0 / DM),
            reads=("psE",), writes=("f_mu",))
        p.op("dve", lambda e, r1=r1: e.tensor_scalar(
            r1, psF[0:1, :], 1.0 / DM, LN_EPS, OP.mult, OP.add),
            reads=("psF",), writes=("f_rw",))
        p.op("dve", lambda e, tsl=tsl, r2=r2: e.scalar_tensor_tensor(
            r2, mu2[0:1, tsl], -1.0, mu2[0:1, tsl],
            OP.mult, OP.mult),
            reads=("f_mu",), writes=("f_rw",))
        p.op("dve", lambda e, r1=r1, r2=r2: e.tensor_add(r1, r1, r2),
             reads=("f_rw",), writes=("f_rw",))
        p.op("act", lambda e, r1=r1: e.activation(r1, r1, AF.Sqrt),
             reads=("f_rw",), writes=("f_rw",))
        p.op("dve", lambda e, r1=r1: e.reciprocal(r1, r1),
             reads=("f_rw",), writes=("f_rw",))
        p.op("dve", lambda e, tsl=tsl, r1=r1: e.tensor_copy(
            rs2[0:1, tsl], r1),
            reads=("f_rw",), writes=("f_rs",))
        p.op("pe", lambda e, tsl=tsl: e.matmul(
            psE[:, :], onesrb[0:1, :], mu2[0:1, tsl], start=True, stop=True),
            reads=("onesrb", "f_mu"), writes=("psE",))
        p.op("pe", lambda e, tsl=tsl: e.matmul(
            psF[:, :], onesrb[0:1, :], rs2[0:1, tsl], start=True, stop=True),
            reads=("onesrb", "f_rs"), writes=("psF",))
        for pc in range(8):
            sl = slice(pc * 512, (pc + 1) * 512)
            p.op("dve", lambda e, sl=sl: e.tensor_sub(
                xt8[:, sl], xt8[:, sl], psE[:, :]),
                reads=("xt8", "psE"), writes=("xt8",))
            p.op("dve", lambda e, sl=sl: e.tensor_mul(
                xt8[:, sl], xt8[:, sl], psF[:, :]),
                reads=("xt8", "psF"), writes=("xt8",))
            p.op("dve", lambda e, sl=sl, pc=pc, t4=t4: e.tensor_scalar(
                xnsh[:, pc * TC + t4 * 512: pc * TC + (t4 + 1) * 512],
                xt8[:, sl], lngc[:, pc:pc + 1], lnbc[:, pc:pc + 1],
                OP.mult, OP.add),
                reads=("xt8", "lngc", "lnbc"), writes=("xnsh",))
    jgs = [(0, 8), (8, 16), (16, 22)]
    for gi, (j0, j1) in enumerate(jgs):
        nj = j1 - j0
        p.dma("poolq", wff[:, 0: nj * 1024],
              w1_d[:, j0 * 1024: j1 * 1024], reads=("wff",), writes=("wff",))
        p.dma("poolq", wff[:, 64 * 128: 64 * 128 + nj * 1024],
              w1_d[:, 22 * 1024 + j0 * 1024: 22 * 1024 + j1 * 1024],
              reads=("wff",), writes=("wff",))
        p.dma("poolq", wk2[:, 0: nj * 1024],
              w2_d[:, j0 * 1024: j1 * 1024], reads=("wk2",), writes=("wk2",))
        for t4 in range(4):
            if gi > 0:
                p.dma("spq",
                      pt8[:, :].rearrange("p (pc w) -> p pc w", pc=8),
                      h2p_r[:, :, t4 * 512:(t4 + 1) * 512],
                      reads=("ptile8",), writes=("ptile8",))
            for j in range(j0, j1):
                jj = j - j0
                pairs = [(wff[:, (jj * 8 + kc) * 128:(jj * 8 + kc + 1) * 128],
                          "wff",
                          xnsh[:, kc * TC + t4 * 512: kc * TC + (t4 + 1) * 512],
                          "xnsh") for kc in range(8)]
                emit_mm_acc(p, psA[:, :], "psA", pairs)
                pairs = [(wff[:, (64 + jj * 8 + kc) * 128:
                              (64 + jj * 8 + kc + 1) * 128], "wff",
                          xnsh[:, kc * TC + t4 * 512: kc * TC + (t4 + 1) * 512],
                          "xnsh") for kc in range(8)]
                emit_mm_acc(p, psB[:, :], "psB", pairs)
                p.op("act", lambda e, j=j: e.activation(
                    lnzb[:, :], psB[:, :], AF.Gelu, bias=b1c[:, 22 + j:23 + j]),
                    reads=("psB", "b1c"), writes=("f_zb",))
                p.op("dve", lambda e, j=j, jj=jj: e.scalar_tensor_tensor(
                    agb[:, jj * 512:(jj + 1) * 512], psA[:, :],
                    b1c[:, j:j + 1], lnzb[:, :], OP.add, OP.mult),
                    reads=("psA", "b1c", "f_zb"), writes=("agb",))
            for nc_ in range(8):
                pairs = [(wk2[:, (jj2 * 8 + nc_) * 128:
                              (jj2 * 8 + nc_ + 1) * 128], "wk2",
                          agb[:, jj2 * 512:(jj2 + 1) * 512], "agb")
                         for jj2 in range(nj)]
                emit_mm_acc(p, psA[:, :], "psA", pairs)
                psl = slice(nc_ * 512, (nc_ + 1) * 512)
                if gi == 0:
                    p.op("dve", lambda e, psl=psl: e.tensor_copy(
                        pt8[:, psl], psA[:, :]),
                        reads=("psA",), writes=("ptile8",))
                else:
                    p.op("dve", lambda e, psl=psl: e.tensor_add(
                        pt8[:, psl], pt8[:, psl], psA[:, :]),
                        reads=("ptile8", "psA"), writes=("ptile8",))
            if gi < 2:
                p.dma("spq", h2p_r[:, :, t4 * 512:(t4 + 1) * 512],
                      pt8[:, :].rearrange("p (pc w) -> p pc w", pc=8),
                      reads=("ptile8",), writes=("h2p",))
            else:
                out_tile(t4, pt8, "ptile8")


def build_l1a(debug=False):
    nc = bass.Bass("TRN2", target_bir_lowering=False, debug=False)
    p = P(nc)

    def DI(n, shp):
        return nc.dram_tensor(n, shp, F32, kind="ExternalInput").ap()

    x1e_d = DI("x1e", [DM, WEXT])
    wq_d = DI("Wq", [128, 8 * 1024]); bq_d = DI("bq", [128, 8])
    wk_d = DI("Wk", [128, 8 * 1024]); bk_d = DI("bk", [128, 8])
    wv_d = DI("Wv", [128, 8 * 1024]); bv_d = DI("bv", [1, DM])
    wo_d = DI("Wo", [128, 64 * 128]); bo_d = DI("bo", [128, 8])
    pb_d = DI("pos_bias", [128, 16 * 2048])
    wqkv_d = DI("wqkv", [128, 96 * 128])
    qs_d = DI("q_scale", [128, 1]); ks_d = DI("k_scale", [128, 1])
    wo1_d = DI("wo1", [128, 32 * 128])
    lng_d = DI("lng1", [128, 8]); lnb_d = DI("lnb1", [128, 8])
    w1_d = DI("w1p", [128, 2 * 22 * 1024])
    b1_d = DI("b1p", [128, 44])
    w2_d = DI("w2p", [128, 22 * 1024])
    b2_d = DI("b2c", [128, 8])
    identd = DI("ident", [128, 128])
    xc1_d = DI("xc1", [128, WEXT]); xs1_d = DI("xs1", [128, WEXT])
    xc2_d = DI("xc2", [128, WEXT]); xs2_d = DI("xs2", [128, WEXT])
    p32_d = DI("P32", [128, 128])
    tb1_d = DI("TB1", [128, 128]); tb2_d = DI("TB2", [128, 128])
    w0_d = DI("w0row", [1, 512])
    bh8_d = DI("bh8", [128, 32]); bh8T_d = DI("bh8T", [8, 512])
    cfa_d = DI("cntfA", [1, TC]); cfb_d = DI("cntfB", [1, TC])
    cft_d = DI("cntfT", [1, TC])
    s1_d = nc.dram_tensor("s1", [DM, TC], F32, kind="ExternalOutput").ap()
    xi1b_d = nc.dram_tensor("xi1b", [DM, TC], F32, kind="ExternalOutput").ap()
    ctx_d = nc.dram_tensor("ctx_sp", [128, 8 * TC], BF16).ap()
    xia_d = nc.dram_tensor("xia_sp", [DM, 2176], F32).ap()
    h2p_d = nc.dram_tensor("h2p_sp", [DM, TC], F32).ap()

    ident = p.sb("identb", [128, 128], BF16)
    identf = p.sb("identf", [128, 128], F32)
    onescb = p.sb("onescb", [128, 1], BF16)
    onesrb = p.sb("onesrb", [1, 128], BF16)
    onesrf = p.sb("onesrf", [1, 128], F32)
    murow = p.sb("murow", [1, WEXT], BF16)
    rsrow = p.sb("rsrow", [1, WEXT], BF16)
    P32 = p.sb("P32", [128, 128], BF16)
    TB1 = p.sb("TB1", [128, 128], BF16)
    TB2 = p.sb("TB2", [128, 128], BF16)
    w0r = p.sb("w0r", [1, 512], BF16)
    bh8 = p.sb("bh8", [128, 32], BF16)
    bh8T = p.sb("bh8T", [8, 512], BF16)
    qs_c = p.sb("qs_c", [128, 1], F32)
    ks_c = p.sb("ks_c", [128, 1], F32)
    lngc = p.sb("lngc", [128, 8], F32)
    lnbc = p.sb("lnbc", [128, 8], F32)
    b2c = p.sb("b2c", [128, 8], F32)
    b1c = p.sb("b1c", [128, 44], F32)
    boc = p.sb("boc", [128, 8], F32)
    p.dma("poolq", ident[:, :], identd, writes=("ident",))
    p.dma("spq", identf[:, :], identd, writes=("identf",))
    p.op("dve", lambda e: e.memset(onescb[:, :], 1.0), writes=("onescb",))
    p.op("dve", lambda e: e.memset(onesrb[:, :], 1.0), writes=("onesrb",))
    p.op("dve", lambda e: e.memset(onesrf[:, :], 1.0), writes=("onesrf",))
    p.dma("poolq", P32[:, :], p32_d, writes=("P32",))
    p.dma("poolq", TB1[:, :], tb1_d, writes=("TB1",))
    p.dma("poolq", TB2[:, :], tb2_d, writes=("TB2",))
    p.dma("poolq", w0r[:, :], w0_d, writes=("w0r",))
    p.dma("poolq", bh8[:, :], bh8_d, writes=("bh8",))
    p.dma("poolq", bh8T[:, :], bh8T_d, writes=("bh8T",))
    p.dma("spq", qs_c[:, :], qs_d, writes=("qs_col",))
    p.dma("spq", ks_c[:, :], ks_d, writes=("ks_col",))
    p.dma("spq", lngc[:, :], lng_d, writes=("lngc",))
    p.dma("spq", lnbc[:, :], lnb_d, writes=("lnbc",))
    p.dma("spq", b2c[:, :], b2_d, writes=("b2c",))
    p.dma("spq", b1c[:, :], b1_d, writes=("b1c",))
    p.dma("spq", boc[:, :], bo_d, writes=("boc",))

    psb1 = p.ps("psb1", [128, 1024], BF16)
    psA = p.ps("psA", [128, 512], F32)
    psB = p.ps("psB", [128, 512], F32)
    psC = p.ps("psC", [128, 512], F32)
    psD = p.ps("psD", [128, 512], F32)
    psE = p.ps("psE", [128, 512], F32)
    psF = p.ps("psF", [128, 512], F32)
    psG = p.ps("psG", [128, 512], F32)

    lnx = p.sb("lnx", [128, 512], F32)
    lnzb = p.sb("lnzb", [128, 512], BF16)
    lnzq = p.sb("lnzq", [128, 512], BF16)
    lnrw = p.sb("lnrw", [1, 1024], F32)
    xt8 = p.sb("xt8", [128, 8 * 512], F32)
    emit_ln_rows(p, x1e_d, WEXT, murow, rsrow, lnx, lnzb, lnzq, psE, psF,
                 onescb, "ln_", rws=lnrw, xt8=xt8, xt8key="xt8")
    lns = LnSrc(p, x1e_d, murow, rsrow, onesrb, psF, psG, lnx, "ln_",
                kMu="psF", kRs="psG")

    XA0 = LEXT - 256
    xn1e = p.sb("xn1e", [128, 8 * 2560], BF16)
    for et4 in range(5):
        col0 = XA0 + et4 * 512
        lns.group_load(col0, 512, xt8, "xt8")
        for pc in range(8):
            lns.norm_pc(pc, col0, 512, xt8, "xt8",
                        xn1e[:, pc * 2560 + et4 * 512:
                             pc * 2560 + (et4 + 1) * 512], "xn1e")

    wqb = p.sb("wqb", [128, 2 * 8 * 128], BF16)
    wkb = p.sb("wkb", [128, 2 * 8 * 128], BF16)
    wvb = p.sb("wvb", [128, 2 * 8 * 128], BF16)
    bqc = p.sb("bqc", [128, 8], F32)
    bkc = p.sb("bkc", [128, 8], F32)
    bvr = p.sb("bvr", [1, DM], BF16)
    pb2 = p.sb("pb2", [128, 4096], BF16)
    q2 = p.sb("q2", [128, TC], BF16)
    k2 = p.sb("k2", [128, 2560], BF16)
    v2 = p.sb("v2", [128, 20 * 128], BF16)
    fA = p.sb("fA", [128, TC], BF16)
    fB = p.sb("fB", [128, TC], BF16)
    fT = p.sb("fT", [128, TC], BF16)
    mx1 = p.sb("mx1", [128, 8], F32)
    prbN = p.sb("prbN", [128, 2 * 512], BF16)
    prbT = p.sb("prbT", [128, 2 * 512], BF16)
    cfr = p.sb("cfr", [1, 3 * TC], F32)
    ctxs = p.sb("ctxs", [128, TC], BF16)
    scr2 = p.sb("scr2", [128, 512], BF16)
    p.dma("spq", bqc[:, :], bq_d, writes=("bqc",))
    p.dma("spq", bkc[:, :], bk_d, writes=("bkc",))
    p.dma("poolq", bvr[:, :], bv_d, writes=("bvr",))
    p.dma("spq", cfr[0:1, 0:TC], cfa_d, writes=("cfr",))
    p.dma("spq", cfr[0:1, TC:2 * TC], cfb_d, writes=("cfr",))
    p.dma("spq", cfr[0:1, 2 * TC:3 * TC], cft_d, writes=("cfr",))

    for hp in range(8):
        hpar = hp % 2
        W0 = hpar * 8 * 128
        kwq, kwk, kwv = "wqb%d" % hpar, "wkb%d" % hpar, "wvb%d" % hpar
        for wsrc, wbuf, wkey in ((wq_d, wqb, kwq), (wk_d, wkb, kwk),
                                 (wv_d, wvb, kwv)):
            p.dma("poolq", wbuf[:, W0: W0 + 1024],
                  wsrc[:, hp * 1024:(hp + 1) * 1024],
                  reads=(wkey,), writes=(wkey,))
        p.dma("poolq", pb2[:, :], pb_d[:, 2 * hp * 2048:(2 * hp + 2) * 2048],
              reads=("pb2",), writes=("pb2",))
        for t4 in range(4):
            psX, kX = ((psA, "psA"), (psB, "psB"))[t4 % 2]
            pairs = [(wqb[:, W0 + kc * 128: W0 + (kc + 1) * 128], kwq,
                      xn1e[:, kc * 2560 + 256 + t4 * 512:
                           kc * 2560 + 256 + (t4 + 1) * 512], "xn1e")
                     for kc in range(8)]
            emit_mm_acc(p, psX[:, :], kX, pairs)
            p.op("dve", lambda e, t4=t4, hp=hp, psX=psX: e.tensor_scalar(
                q2[:, t4 * 512:(t4 + 1) * 512], psX[:, :],
                bqc[:, hp:hp + 1], 0.125, OP.add, OP.mult),
                reads=(kX, "bqc"), writes=("q2_%d" % t4,))
        for t4 in range(5):
            psX, kX = ((psC, "psC"), (psD, "psD"))[t4 % 2]
            pairs = [(wkb[:, W0 + kc * 128: W0 + (kc + 1) * 128], kwk,
                      xn1e[:, kc * 2560 + t4 * 512: kc * 2560 + (t4 + 1) * 512],
                      "xn1e") for kc in range(8)]
            emit_mm_acc(p, psX[:, :], kX, pairs)
            p.op("dve", lambda e, t4=t4, hp=hp, psX=psX: e.tensor_scalar(
                k2[:, t4 * 512:(t4 + 1) * 512], psX[:, :],
                bkc[:, hp:hp + 1], None, OP.add),
                reads=(kX, "bkc"), writes=("k2_%d" % t4,))
        for tc in range(20):
            psX, kX = ((psE, "psE"), (psG, "psG"))[tc % 2]
            pairs = [(xn1e[:, kc * 2560 + tc * 128: kc * 2560 + (tc + 1) * 128],
                      "xn1e", wvb[:, W0 + kc * 128: W0 + (kc + 1) * 128],
                      kwv) for kc in range(8)]
            for i, (lap, lkey, rap, rkey) in enumerate(pairs):
                p.op("pe", lambda e, lap=lap, rap=rap, i=i, psX=psX: e.matmul(
                    psX[:, 0:128], lap, rap, start=(i == 0), stop=False),
                    reads=(lkey, rkey), writes=(kX,))
            p.op("pe", lambda e, hp=hp, psX=psX: e.matmul(
                psX[:, 0:128], onesrb[0:1, :],
                bvr[0:1, hp * 128:(hp + 1) * 128], start=False, stop=True),
                reads=("onesrb", "bvr"), writes=(kX,))
            p.op("dve", lambda e, tc=tc, psX=psX: e.tensor_copy(
                v2[:, tc * 128:(tc + 1) * 128], psX[:, 0:128]),
                reads=(kX,), writes=("v2_%d" % tc,))
        p.op("dve", lambda e: e.memset(fT[:, :], 0.0), writes=("fT",))
        units = []
        for wi in range(9):
            ws = 256 * (wi - 1)
            for qc in range(4):
                q0 = ws + qc * 128
                if 0 <= q0 < TC:
                    units.append((ws, q0, False))
        for i in range(4):
            units.append((TC - 512, TC - 512 + i * 128, True))
        for ui, (kv0, q0, tail) in enumerate(units):
            qwl = q0 - kv0
            par = ui % 2
            prN = prbN[:, par * 512:(par + 1) * 512]
            prT = prbT[:, par * 512:(par + 1) * 512]
            kprN, kprT = "prbN%d" % par, "prbT%d" % par
            psPV = (psE, psG)[par]
            kPV = ("psE", "psG")[par]
            for h2 in range(2):
                ps = (psA, psB)[h2] if par == 0 else (psC, psD)[h2]
                pk = (("psA", "psB") if par == 0 else ("psC", "psD"))[h2]
                mxs = mx1[:, 4 * par + 2 * h2: 4 * par + 2 * h2 + 1]
                mxd = mx1[:, 4 * par + 2 * h2 + 1: 4 * par + 2 * h2 + 2]
                kmx = "mx1_%d%d" % (par, h2)
                kread = tuple({"k2_%d" % ((kv0 + 256) // 512),
                               "k2_%d" % ((kv0 + 256 + 511) // 512)})
                p.op("pe", lambda e, h2=h2, q0=q0, kv0=kv0, ps=ps: e.matmul(
                    ps[:, :], q2[64 * h2:64 * (h2 + 1), q0:q0 + 128],
                    k2[64 * h2:64 * (h2 + 1), kv0 + 256: kv0 + 256 + 512],
                    start=True, stop=False, tile_position=(64 * h2, 0)),
                    reads=("q2_%d" % (q0 // 512),) + kread, writes=(pk,))
                p.op("pe", lambda e, h2=h2, qwl=qwl, ps=ps: e.matmul(
                    ps[:, :], ident[:, :],
                    pb2[:, h2 * 2048 + (qwl // 128) * 512:
                        h2 * 2048 + (qwl // 128) * 512 + 512],
                    start=False, stop=True),
                    reads=("ident", "pb2"), writes=(pk,))
                p.op("dve", lambda e, ps=ps, mxs=mxs: e.tensor_reduce(
                    mxs, ps[:, :], AX.X, OP.max, negate=True),
                    reads=(pk,), writes=(kmx,))
                p.op("act", lambda e, ps=ps, prN=prN, mxs=mxs, mxd=mxd:
                     e.activation(
                    prN, ps[:, :], AF.Exp, bias=mxs,
                    accum_out=mxd),
                    reads=(pk, kmx), writes=(kprN, kmx))
                p.op("dve", lambda e, mxd=mxd: e.reciprocal(mxd, mxd),
                     reads=(kmx,), writes=(kmx,))
                p.op("dve", lambda e, prN=prN, mxd=mxd: e.tensor_scalar(
                    prN, prN, mxd, None, OP.mult),
                    reads=(kprN, kmx), writes=(kprN,))
                for kb in range(4):
                    p.op("pe", lambda e, kb=kb, prN=prN, par=par: e.transpose(
                        psb1[:, par * 512 + kb * 128:
                             par * 512 + (kb + 1) * 128],
                        prN[:, kb * 128:(kb + 1) * 128], ident[:, :]),
                        reads=(kprN, "ident"), writes=("psb1",))
                p.op("dve", lambda e, prT=prT, par=par: e.tensor_copy(
                    prT, psb1[:, par * 512:(par + 1) * 512]),
                    reads=("psb1",), writes=(kprT,))
                for kb in range(4):
                    tcb = (kv0 + 256) // 128 + kb
                    p.op("pe", lambda e, kb=kb, tcb=tcb, h2=h2, psPV=psPV,
                         prT=prT: e.matmul(
                        psPV[64 * h2:64 * (h2 + 1), 0:128],
                        v2[:, tcb * 128 + 64 * h2: tcb * 128 + 64 * h2 + 64],
                        prT[:, kb * 128:(kb + 1) * 128],
                        start=(kb == 0), stop=(kb == 3),
                        tile_position=(0, 64 * h2)),
                        reads=("v2_%d" % tcb, kprT), writes=(kPV,))
            if tail:
                fb, fk = fT, "fT"
            elif qwl < 256:
                fb, fk = fA, "fA"
            else:
                fb, fk = fB, "fB"
            p.op("dve", lambda e, fb=fb, q0=q0, psPV=psPV: e.tensor_copy(
                fb[:, q0:q0 + 128], psPV[:, 0:128]),
                reads=(kPV,), writes=(fk,))
        for t4 in range(4):
            tsl = slice(t4 * 512, (t4 + 1) * 512)
            for i, (fb, fk) in enumerate(((fA, "fA"), (fB, "fB"), (fT, "fT"))):
                p.op("pe", lambda e, i=i, tsl=tsl: e.matmul(
                    psF[:, :], onesrf[0:1, :],
                    cfr[0:1, i * TC + tsl.start: i * TC + tsl.stop],
                    start=True, stop=True),
                    reads=("onesrf", "cfr"), writes=("psF",))
                if i == 0:
                    p.op("dve", lambda e, fb=fb, tsl=tsl: e.tensor_mul(
                        ctxs[:, tsl], fb[:, tsl], psF[:, :]),
                        reads=(fk, "psF"), writes=("ctxs",))
                else:
                    p.op("dve", lambda e, fb=fb, tsl=tsl: e.tensor_mul(
                        scr2[:, :], fb[:, tsl], psF[:, :]),
                        reads=(fk, "psF"), writes=("scr2",))
                    p.op("dve", lambda e, tsl=tsl: e.tensor_add(
                        ctxs[:, tsl], ctxs[:, tsl], scr2[:, :]),
                        reads=("ctxs", "scr2"), writes=("ctxs",))
        p.dma("spq", ctx_d[:, hp * TC:(hp + 1) * TC], ctxs[:, :],
              reads=("ctxs",), writes=("ctx_d",))
    p.free("scr2", "ctxs", "cfr", "prbT", "prbN", "mx1", "fT", "fB", "fA",
           "v2", "k2", "q2", "pb2", "bvr", "bkc", "bqc", "wvb", "wkb", "wqb")

    wot = p.sb("wot", [128, 64 * 128], BF16)
    cxb = p.sb("cxb", [128, 8 * 512], BF16)
    s1t8 = p.sb("s1t8", [128, 8 * 512], F32)
    p.dma("poolq", wot[:, :], wo_d[:, :], reads=("wot",), writes=("wot",))
    ctx_r = ctx_d.rearrange("p (kc w) -> p kc w", kc=8)
    s1_r = s1_d.rearrange("(pc p) w -> p pc w", p=128)
    for t4 in range(4):
        p.dma("spq", cxb[:, :].rearrange("p (kc w) -> p kc w", kc=8),
              ctx_r[:, :, t4 * 512:(t4 + 1) * 512],
              reads=("ctx_d", "cxb"), writes=("cxb",))
        for pc in range(8):
            pairs = [(wot[:, (kc * 8 + pc) * 128:(kc * 8 + pc + 1) * 128],
                      "wot", cxb[:, kc * 512:(kc + 1) * 512], "cxb")
                     for kc in range(8)]
            emit_mm_acc(p, psA[:, :], "psA", pairs)
            p.op("dve", lambda e, pc=pc: e.tensor_scalar(
                s1t8[:, pc * 512:(pc + 1) * 512], psA[:, :],
                boc[:, pc:pc + 1], None, OP.add),
                reads=("psA", "boc"), writes=("s1t8",))
        p.dma("spq", s1_r[:, :, t4 * 512:(t4 + 1) * 512],
              s1t8[:, :].rearrange("p (pc w) -> p pc w", pc=8),
              reads=("s1t8",), writes=("s1_d",))
    p.free("s1t8", "cxb", "wot")
    p.free("xn1e")

    hblk = p.sb("hblk", [128, 8 * 512], BF16)
    wqt = p.sb("wqt", [128, 96 * 128], BF16)
    qI = p.sb("qI", [128, 4 * 3072], BF16)
    kI = p.sb("kI", [128, 4 * 3072], BF16)
    vI = p.sb("vI", [128, 24 * 512], BF16)
    sqb = p.sb("sqb", [128, 512], BF16)
    rnr = p.sb("rnr", [8, 3072], F32)
    rnb = p.sb("rnb", [8, 3072], BF16)
    xtb = p.sb("xtb", [128, 1024], BF16)
    mxc = p.sb("mxc", [128, 8], F32)
    iprbN = p.sb("iprbN", [128, 2 * 640], BF16)
    iprbT = p.sb("iprbT", [128, 2 * 640], BF16)
    wo_t1 = p.sb("wo_t1", [128, 32 * 128], BF16)
    o_fm = p.sb("o_fm", [128, 4 * 2176], BF16)
    xres8 = p.sb("xres8", [128, 8 * 128], F32)
    p.dma("poolq", wo_t1[:, :], wo1_d[:, :], reads=("wo_t1",),
          writes=("wo_t1",))
    t_inner = dict(psA=psA, psB=psB, psC=psC, psD=psD, psE=psE, psF=psF,
                   psb1=psb1, xt8=xt8,
                   ident=ident, onesrb=onesrb, hblk=hblk, wqt=wqt,
                   qI=qI, kI=kI, vI=vI, sqb=sqb, rnr=rnr, rnb=rnb,
                   bh8=bh8, bh8T=bh8T,
                   xtb=xtb, mxc=mxc, prbN=iprbN, prbT=iprbT,
                   TB1=TB1, TB2=TB2, w0r=w0r, P32=P32, wo_t=wo_t1, o_fm=o_fm)
    xia_r = xia_d.rearrange("(pc p) w -> p pc w", p=128)

    def o_xdst(et, pcout, ps_ap, ps_key):
        col0 = LEXT - 128 + et * 128
        if pcout == 0:
            lns.group_load(col0, 128, xt8, "xt8")
            for pc in range(8):
                lns.norm_pc(pc, col0, 128, xt8, "xt8",
                            xres8[:, pc * 128:(pc + 1) * 128], "xres8")
        p.op("dve", lambda e, ps_ap=ps_ap, pcout=pcout: e.tensor_add(
            xres8[:, pcout * 128:(pcout + 1) * 128],
            xres8[:, pcout * 128:(pcout + 1) * 128], ps_ap),
            reads=("xres8", ps_key), writes=("xres8",))
        if pcout == 7:
            p.dma("spq", xia_r[:, :, et * 128:(et + 1) * 128],
                  xres8[:, :].rearrange("p (pc w) -> p pc w", pc=8),
                  reads=("xres8",), writes=("xia_d",))

    emit_inner_attn(p, t_inner, lns, wqkv_d, qs_c[:, 0:1], ks_c[:, 0:1],
                    (xc1_d, xs1_d, xc2_d, xs2_d), o_xdst)
    if debug:
        p.dump("qI", qI[:, :], ("qI",))
        p.dump("kI", kI[:, :], ("kI",))
        p.dump("vI", vI[:, :], ("vI",))
        p.dump("o_fm", o_fm[:, :], ("o_fm",))
    p.free("xres8", "o_fm", "wo_t1", "iprbT", "iprbN", "mxc", "xtb",
           "rnb", "rnr", "sqb", "vI", "kI", "qI", "wqt", "hblk")

    mu2 = p.sb("mu2", [1, TC], BF16)
    rs2 = p.sb("rs2", [1, TC], BF16)
    f_lnzb = p.sb("f_lnzb", [128, 512], BF16)
    f_lnzq = p.sb("f_lnzq", [128, 512], BF16)
    f_rws = p.sb("f_rws", [1, 1024], F32)
    xnsh = p.sb("xnsh", [128, 8 * TC], BF16)
    wff = p.sb("wff", [128, 128 * 128], BF16)
    wk2 = p.sb("wk2", [128, 64 * 128], BF16)
    agb = p.sb("agb", [128, 8 * 512], BF16)
    ptile8 = p.sb("ptile8", [128, 8 * 512], F32)
    t_ffn = dict(psA=psA, psB=psB, psE=psE, psF=psF, onescb=onescb,
                 onesrb=onesrb, mu2=mu2, rs2=rs2, lnzb=f_lnzb,
                 lnzq=f_lnzq, rws=f_rws, xnsh=xnsh, wff=wff, wk2=wk2,
                 agb=agb, b1c=b1c, h2p_d=h2p_d, ptile8=ptile8, xt8=xt8)
    xi1b_r = xi1b_d.rearrange("(pc p) w -> p pc w", p=128)

    def out_t4(t4, pt8, ptk):
        p.dma("spq", xt8[:, :].rearrange("p (pc w) -> p pc w", pc=8),
              xia_r[:, :, 128 + t4 * 512: 128 + (t4 + 1) * 512],
              reads=("xt8",), writes=("xt8",))
        for nc_ in range(8):
            sl = slice(nc_ * 512, (nc_ + 1) * 512)
            p.op("dve", lambda e, sl=sl, nc_=nc_, pt8=pt8:
                 e.scalar_tensor_tensor(
                xt8[:, sl], pt8[:, sl], b2c[:, nc_:nc_ + 1], xt8[:, sl],
                OP.add, OP.add),
                reads=(ptk, "b2c", "xt8"), writes=("xt8",))
        p.dma("spq", xi1b_r[:, :, t4 * 512:(t4 + 1) * 512],
              xt8[:, :].rearrange("p (pc w) -> p pc w", pc=8),
              reads=("xt8",), writes=("xi1b_d",))

    emit_inner_ffn(p, t_ffn, xia_d, w1_d, b1_d, w2_d, b2c, lngc, lnbc,
                   out_t4)
    p.finalize()
    return nc


def build_l1b(debug=False):
    nc = bass.Bass("TRN2", target_bir_lowering=False, debug=False)
    p = P(nc)

    def DI(n, shp):
        return nc.dram_tensor(n, shp, F32, kind="ExternalInput").ap()

    xie_d = DI("xie", [DM, WEXT])
    x1_d = DI("x1c", [DM, TC])
    s1_d = DI("s1c", [DM, TC])
    wqkv_d = DI("wqkv", [128, 96 * 128])
    qs_d = DI("q_scale", [128, 1]); ks_d = DI("k_scale", [128, 1])
    wo1_d = DI("wo1", [128, 32 * 128])
    lng_d = DI("lng1", [128, 8]); lnb_d = DI("lnb1", [128, 8])
    w1_d = DI("w1p", [128, 2 * 22 * 1024])
    b1_d = DI("b1p", [128, 44])
    w2_d = DI("w2p", [128, 22 * 1024])
    b2_d = DI("b2c", [128, 8])
    identd = DI("ident", [128, 128])
    xc1_d = DI("xc1", [128, WEXT]); xs1_d = DI("xs1", [128, WEXT])
    xc2_d = DI("xc2", [128, WEXT]); xs2_d = DI("xs2", [128, WEXT])
    p32_d = DI("P32", [128, 128])
    tb1_d = DI("TB1", [128, 128]); tb2_d = DI("TB2", [128, 128])
    w0_d = DI("w0row", [1, 512])
    bh8_d = DI("bh8", [128, 32]); bh8T_d = DI("bh8T", [8, 512])
    out_d = nc.dram_tensor("outp", [DM, TC], F32, kind="ExternalOutput").ap()
    xia_d = nc.dram_tensor("xia2_sp", [DM, 2176], F32).ap()
    h2p_d = nc.dram_tensor("h2p2_sp", [DM, TC], F32).ap()

    ident = p.sb("identb", [128, 128], BF16)
    onescb = p.sb("onescb", [128, 1], BF16)
    onesrb = p.sb("onesrb", [1, 128], BF16)
    P32 = p.sb("P32", [128, 128], BF16)
    TB1 = p.sb("TB1", [128, 128], BF16)
    TB2 = p.sb("TB2", [128, 128], BF16)
    w0r = p.sb("w0r", [1, 512], BF16)
    bh8 = p.sb("bh8", [128, 32], BF16)
    bh8T = p.sb("bh8T", [8, 512], BF16)
    qs_c = p.sb("qs_c", [128, 1], F32)
    ks_c = p.sb("ks_c", [128, 1], F32)
    lngc = p.sb("lngc", [128, 8], F32)
    lnbc = p.sb("lnbc", [128, 8], F32)
    b2c = p.sb("b2c", [128, 8], F32)
    b1c = p.sb("b1c", [128, 44], F32)
    p.dma("poolq", ident[:, :], identd, writes=("ident",))
    p.op("dve", lambda e: e.memset(onescb[:, :], 1.0), writes=("onescb",))
    p.op("dve", lambda e: e.memset(onesrb[:, :], 1.0), writes=("onesrb",))
    p.dma("poolq", P32[:, :], p32_d, writes=("P32",))
    p.dma("poolq", TB1[:, :], tb1_d, writes=("TB1",))
    p.dma("poolq", TB2[:, :], tb2_d, writes=("TB2",))
    p.dma("poolq", w0r[:, :], w0_d, writes=("w0r",))
    p.dma("poolq", bh8[:, :], bh8_d, writes=("bh8",))
    p.dma("poolq", bh8T[:, :], bh8T_d, writes=("bh8T",))
    p.dma("spq", qs_c[:, :], qs_d, writes=("qs_col",))
    p.dma("spq", ks_c[:, :], ks_d, writes=("ks_col",))
    p.dma("spq", lngc[:, :], lng_d, writes=("lngc",))
    p.dma("spq", lnbc[:, :], lnb_d, writes=("lnbc",))
    p.dma("spq", b2c[:, :], b2_d, writes=("b2c",))
    p.dma("spq", b1c[:, :], b1_d, writes=("b1c",))

    psb1 = p.ps("psb1", [128, 1024], BF16)
    psA = p.ps("psA", [128, 512], F32)
    psB = p.ps("psB", [128, 512], F32)
    psC = p.ps("psC", [128, 512], F32)
    psD = p.ps("psD", [128, 512], F32)
    psE = p.ps("psE", [128, 512], F32)
    psF = p.ps("psF", [128, 512], F32)

    lns = LnSrc(p, xie_d, None, None, None, None, None, None, "r_")

    hblk = p.sb("hblk", [128, 8 * 512], BF16)
    wqt = p.sb("wqt", [128, 96 * 128], BF16)
    qI = p.sb("qI", [128, 4 * 3072], BF16)
    kI = p.sb("kI", [128, 4 * 3072], BF16)
    vI = p.sb("vI", [128, 24 * 512], BF16)
    sqb = p.sb("sqb", [128, 512], BF16)
    rnr = p.sb("rnr", [8, 3072], F32)
    rnb = p.sb("rnb", [8, 3072], BF16)
    xtb = p.sb("xtb", [128, 1024], BF16)
    mxc = p.sb("mxc", [128, 8], F32)
    iprbN = p.sb("iprbN", [128, 2 * 640], BF16)
    iprbT = p.sb("iprbT", [128, 2 * 640], BF16)
    wo_t1 = p.sb("wo_t1", [128, 32 * 128], BF16)
    o_fm = p.sb("o_fm", [128, 4 * 2176], BF16)
    xres8 = p.sb("xres8", [128, 8 * 128], F32)
    p.dma("poolq", wo_t1[:, :], wo1_d[:, :], reads=("wo_t1",),
          writes=("wo_t1",))
    t_inner = dict(psA=psA, psB=psB, psC=psC, psD=psD, psE=psE, psF=psF,
                   psb1=psb1,
                   ident=ident, onesrb=onesrb, hblk=hblk, wqt=wqt,
                   qI=qI, kI=kI, vI=vI, sqb=sqb, rnr=rnr, rnb=rnb,
                   bh8=bh8, bh8T=bh8T,
                   xtb=xtb, mxc=mxc, prbN=iprbN, prbT=iprbT,
                   TB1=TB1, TB2=TB2, w0r=w0r, P32=P32, wo_t=wo_t1, o_fm=o_fm)
    xie_r = xie_d.rearrange("(pc p) w -> p pc w", p=128)
    xia_r = xia_d.rearrange("(pc p) w -> p pc w", p=128)

    def o_xdst(et, pcout, ps_ap, ps_key):
        col0 = LEXT - 128 + et * 128
        if pcout == 0:
            p.dma("spq", xres8[:, :].rearrange("p (pc w) -> p pc w", pc=8),
                  xie_r[:, :, col0:col0 + 128],
                  reads=("xres8",), writes=("xres8",))
        p.op("dve", lambda e, ps_ap=ps_ap, pcout=pcout: e.tensor_add(
            xres8[:, pcout * 128:(pcout + 1) * 128],
            xres8[:, pcout * 128:(pcout + 1) * 128], ps_ap),
            reads=("xres8", ps_key), writes=("xres8",))
        if pcout == 7:
            p.dma("spq", xia_r[:, :, et * 128:(et + 1) * 128],
                  xres8[:, :].rearrange("p (pc w) -> p pc w", pc=8),
                  reads=("xres8",), writes=("xia_d",))

    emit_inner_attn(p, t_inner, lns, wqkv_d, qs_c[:, 0:1], ks_c[:, 0:1],
                    (xc1_d, xs1_d, xc2_d, xs2_d), o_xdst)
    if debug:
        p.dump("xia2", xia_d[:, :], ("xia_d",))
    p.free("xres8", "o_fm", "wo_t1", "iprbT", "iprbN", "mxc", "xtb",
           "rnb", "rnr", "sqb", "vI", "kI", "qI", "wqt", "hblk")

    mu2 = p.sb("mu2", [1, TC], BF16)
    rs2 = p.sb("rs2", [1, TC], BF16)
    f_lnzb = p.sb("f_lnzb", [128, 512], BF16)
    f_lnzq = p.sb("f_lnzq", [128, 512], BF16)
    f_rws = p.sb("f_rws", [1, 1024], F32)
    xnsh = p.sb("xnsh", [128, 8 * TC], BF16)
    wff = p.sb("wff", [128, 128 * 128], BF16)
    wk2 = p.sb("wk2", [128, 64 * 128], BF16)
    agb = p.sb("agb", [128, 8 * 512], BF16)
    ptile8 = p.sb("ptile8", [128, 8 * 512], F32)
    xt8 = p.sb("xt8", [128, 8 * 512], F32)
    xot8 = p.sb("xot8", [128, 8 * 512], F32)
    gsl8 = p.sb("gsl8", [128, 8 * 512], BF16)
    t_ffn = dict(psA=psA, psB=psB, psE=psE, psF=psF, onescb=onescb,
                 onesrb=onesrb, mu2=mu2, rs2=rs2, lnzb=f_lnzb,
                 lnzq=f_lnzq, rws=f_rws, xnsh=xnsh, wff=wff, wk2=wk2,
                 agb=agb, b1c=b1c, h2p_d=h2p_d, ptile8=ptile8, xt8=xt8)
    s1_r = s1_d.rearrange("(pc p) w -> p pc w", p=128)
    x1_r = x1_d.rearrange("(pc p) w -> p pc w", p=128)
    out_r = out_d.rearrange("(pc p) w -> p pc w", p=128)

    def out_t4(t4, pt8, ptk):
        p.dma("spq", xot8[:, :].rearrange("p (pc w) -> p pc w", pc=8),
              xia_r[:, :, 128 + t4 * 512: 128 + (t4 + 1) * 512],
              reads=("xot8",), writes=("xot8",))
        for nc_ in range(8):
            sl = slice(nc_ * 512, (nc_ + 1) * 512)
            p.op("dve", lambda e, sl=sl, nc_=nc_, pt8=pt8:
                 e.scalar_tensor_tensor(
                xot8[:, sl], pt8[:, sl], b2c[:, nc_:nc_ + 1], xot8[:, sl],
                OP.add, OP.add),
                reads=(ptk, "b2c", "xot8"), writes=("xot8",))
        p.op("act", lambda e: e.activation(gsl8[:, :], xot8[:, :], AF.Silu),
             reads=("xot8",), writes=("gsl8",))
        p.dma("spq", xt8[:, :].rearrange("p (pc w) -> p pc w", pc=8),
              s1_r[:, :, t4 * 512:(t4 + 1) * 512],
              reads=("xt8",), writes=("xt8",))
        p.op("dve", lambda e: e.tensor_mul(xot8[:, :], xt8[:, :], gsl8[:, :]),
             reads=("xot8", "xt8", "gsl8"), writes=("xot8",))
        p.dma("spq", xt8[:, :].rearrange("p (pc w) -> p pc w", pc=8),
              x1_r[:, :, t4 * 512:(t4 + 1) * 512],
              reads=("xt8",), writes=("xt8",))
        p.op("dve", lambda e: e.tensor_add(xot8[:, :], xot8[:, :], xt8[:, :]),
             reads=("xot8", "xt8"), writes=("xot8",))
        p.dma("spq", out_r[:, :, t4 * 512:(t4 + 1) * 512],
              xot8[:, :].rearrange("p (pc w) -> p pc w", pc=8),
              reads=("xot8",), writes=("out_d",))

    emit_inner_ffn(p, t_ffn, xia_d, w1_d, b1_d, w2_d, b2c, lngc, lnbc,
                   out_t4)
    if debug:
        p.dump("xnsh2", xnsh[:, :], ("xnsh",))
        p.dump("h2p2", h2p_d[:, :], ("h2p",))
    p.finalize()
    return nc



def host_tables_l1(core):
    tl = {}
    tl["ident"] = np.eye(128, dtype=np.float32)
    pos = (core * TC) + np.arange(WEXT) - LEXT
    half_idx = np.arange(0, 64, 2).astype(np.float64)
    inv_freq = 1.0 / 10000.0 ** (half_idx / 64.0)
    fr = pos[None, :].astype(np.float64) * inv_freq[:, None]
    base = (half_idx + 0.4 * 64) / (1.4 * 64)
    power = (pos.astype(np.float64) - TFULL // 2) / 256.0
    xsc = base[:, None] ** power[None, :]
    cos64 = np.concatenate([np.cos(fr), np.cos(fr)], 0)
    sin64 = np.concatenate([np.sin(fr), np.sin(fr)], 0)
    xsc64 = np.concatenate([xsc, xsc], 0)
    tl["xc1"] = np.tile(cos64 * xsc64, (2, 1)).astype(np.float32)
    tl["xs1"] = np.tile(sin64 * xsc64, (2, 1)).astype(np.float32)
    tl["xc2"] = np.tile(cos64 / xsc64, (2, 1)).astype(np.float32)
    tl["xs2"] = np.tile(sin64 / xsc64, (2, 1)).astype(np.float32)
    P = np.zeros((128, 128), np.float32)
    for b in range(2):
        for d in range(32):
            P[64 * b + d + 32, 64 * b + d] = -1.0
            P[64 * b + d, 64 * b + d + 32] = 1.0
    tl["P32"] = P
    ii = np.arange(128)
    tl["TB1"] = (NEG * (ii[None, :] < ii[:, None])).astype(np.float32)
    tl["TB2"] = (NEG * (ii[None, :] > ii[:, None])).astype(np.float32)
    tl["w0row"] = np.full((1, 512), NEG if core == 0 else 0.0, np.float32)
    bh8 = np.zeros((128, 32), np.float32)
    bh8T = np.zeros((8, 512), np.float32)
    for pc in range(4):
        for r in range(128):
            h = 2 * pc + r // 64
            bh8[r, pc * 8 + h] = 1.0
            bh8T[h, pc * 128 + r] = 1.0
    tl["bh8"] = bh8
    tl["bh8T"] = bh8T
    tg = core * TC + np.arange(TC)
    counts = np.full(TC, 2.0)
    counts[tg < 256] = 1.0
    counts[(tg >= TFULL - 512) & (tg < TFULL - 256)] = 3.0
    counts[tg >= TFULL - 256] = 2.0
    flagA = (tg < TFULL - 256).astype(np.float64)
    flagB = (tg >= 256).astype(np.float64)
    flagT = (tg >= TFULL - 512).astype(np.float64)
    tl["cntfA"] = (flagA / counts).astype(np.float32).reshape(1, TC)
    tl["cntfB"] = (flagB / counts).astype(np.float32).reshape(1, TC)
    tl["cntfT"] = (flagT / counts).astype(np.float32).reshape(1, TC)
    return tl


def _l1_shared_w(inputs, layer):
    f32 = lambda k: np.ascontiguousarray(np.asarray(inputs[k], np.float32))
    w1 = f32("L1_ffn_w1")[layer]
    w1a = np.zeros((DM, 22 * 128), np.float32)
    w1a[:, 0:2730] = w1[:, 0:2730]
    w1g = np.zeros((DM, 22 * 128), np.float32)
    w1g[:, 0:2730] = w1[:, 2730:5460]
    def tile_kn(w, njb):
        return np.ascontiguousarray(
            w.reshape(8, 128, njb, 128).transpose(1, 2, 0, 3
                                                  ).reshape(128, njb * 1024))
    w1p = np.concatenate([tile_kn(w1a, 22), tile_kn(w1g, 22)], axis=1)
    b1 = f32("L1_ffn_b1")[layer]
    b1p = np.zeros((128, 44), np.float32)
    for j in range(22):
        n = min(128, 2730 - j * 128)
        if n > 0:
            b1p[0:n, j] = b1[j * 128: j * 128 + n]
            b1p[0:n, 22 + j] = b1[2730 + j * 128: 2730 + j * 128 + n]
    w2 = f32("L1_ffn_w2")[layer]
    w2f = np.zeros((22 * 128, DM), np.float32)
    w2f[0:2730] = w2
    w2p = np.ascontiguousarray(
        w2f.reshape(22, 128, 8, 128).transpose(1, 0, 2, 3
                                               ).reshape(128, 22 * 1024))
    wqkv = f32("L1_ffn_wqkv")[layer]
    wqkvt = np.ascontiguousarray(
        wqkv.reshape(8, 128, 12, 128).transpose(1, 2, 0, 3
                                                ).reshape(128, 96 * 128))
    wo1 = f32("L1_ffn_wo")[layer]
    wo1t = np.ascontiguousarray(
        wo1.reshape(4, 128, 8, 128).transpose(1, 2, 0, 3
                                              ).reshape(128, 32 * 128))
    qs = f32("L1_ffn_q_scale")[layer]
    ks = f32("L1_ffn_k_scale")[layer]
    return {
        "wqkv": wqkvt,
        "q_scale": np.ascontiguousarray(np.tile(qs, 2).reshape(128, 1)),
        "k_scale": np.ascontiguousarray(np.tile(ks, 2).reshape(128, 1)),
        "wo1": wo1t,
        "lng1": np.ascontiguousarray(f32("L1_ffn_ln_g")[layer].reshape(8, 128).T),
        "lnb1": np.ascontiguousarray(f32("L1_ffn_ln_b")[layer].reshape(8, 128).T),
        "w1p": w1p, "b1p": b1p, "w2p": w2p,
        "b2c": np.ascontiguousarray(f32("L1_ffn_b2")[layer].reshape(8, 128).T),
    }


_L1A_CACHE = [None]
_L1B_CACHE = [None]


def run_l1a(inputs, x1):
    if _L1A_CACHE[0] is None:
        _L1A_CACHE[0] = build_l1a(debug=bool(int(
            os.environ.get("KDEBUG", "0"))))
    nc = _L1A_CACHE[0]
    f32 = lambda k: np.ascontiguousarray(np.asarray(inputs[k], np.float32))
    pb = f32("L1_attn_pos_bias")
    pbh = np.ascontiguousarray(
        pb.reshape(16, 4, 128, 512).transpose(2, 0, 1, 3).reshape(128, 16 * 2048))
    def hp_tiles(w):
        return np.ascontiguousarray(
            np.asarray(w, np.float32).reshape(8, 128, 8, 128)
            .transpose(1, 2, 0, 3).reshape(128, 64 * 128))
    wq_t = hp_tiles(inputs["L1_attn_Wq"])
    wk_t = hp_tiles(inputs["L1_attn_Wk"])
    wv_t = hp_tiles(inputs["L1_attn_Wv"])
    wo_t = np.ascontiguousarray(
        np.asarray(inputs["L1_attn_Wo"], np.float32).reshape(8, 128, 8, 128)
        .transpose(1, 0, 2, 3).reshape(128, 64 * 128))
    shared = _l1_shared_w(inputs, 0)
    in_maps = []
    for c in range(NCORES):
        tl = host_tables_l1(c)
        x1e = np.zeros((DM, WEXT), np.float32)
        x1e[:, LEXT:LEXT + TC] = x1[c]
        if c > 0:
            x1e[:, 0:LEXT] = x1[c - 1][:, TC - LEXT:]
        if c < 7:
            x1e[:, LEXT + TC:LEXT + TC + 256] = x1[c + 1][:, 0:256]
        im = {
            "x1e": x1e,
            "Wq": wq_t, "Wk": wk_t, "Wv": wv_t, "Wo": wo_t,
            "bq": np.ascontiguousarray(f32("L1_attn_bq").reshape(8, 128).T),
            "bk": np.ascontiguousarray(f32("L1_attn_bk").reshape(8, 128).T),
            "bv": f32("L1_attn_bv").reshape(1, DM),
            "bo": np.ascontiguousarray(f32("L1_attn_bo").reshape(8, 128).T),
            "pos_bias": pbh,
        }
        im.update(shared)
        im.update({k: tl[k] for k in ("ident", "xc1", "xs1", "xc2", "xs2",
                                      "P32", "TB1", "TB2", "w0row", "bh8",
                                      "bh8T", "cntfA", "cntfB", "cntfT")})
        in_maps.append(im)
    res = run_bass_kernel_spmd(nc, in_maps, core_ids=list(range(NCORES)))
    run_l1a.last = res
    s1 = np.stack([res.results[c]["s1"] for c in range(NCORES)])
    xi1b = np.stack([res.results[c]["xi1b"] for c in range(NCORES)])
    return s1, xi1b


def run_l1b(inputs, x1, s1, xi1b):
    if _L1B_CACHE[0] is None:
        _L1B_CACHE[0] = build_l1b(debug=bool(int(
            os.environ.get("KDEBUG", "0"))))
    nc = _L1B_CACHE[0]
    shared = _l1_shared_w(inputs, 1)
    in_maps = []
    for c in range(NCORES):
        tl = host_tables_l1(c)
        xie = np.zeros((DM, WEXT), np.float32)
        xie[:, LEXT:LEXT + TC] = xi1b[c]
        if c > 0:
            xie[:, 0:LEXT] = xi1b[c - 1][:, TC - LEXT:]
        im = {"xie": xie, "x1c": np.ascontiguousarray(x1[c]),
              "s1c": np.ascontiguousarray(s1[c])}
        im.update(shared)
        im.update({k: tl[k] for k in ("ident", "xc1", "xs1", "xc2", "xs2",
                                      "P32", "TB1", "TB2", "w0row", "bh8",
                                      "bh8T")})
        in_maps.append(im)
    res = run_bass_kernel_spmd(nc, in_maps, core_ids=list(range(NCORES)))
    run_l1b.last = res
    return np.stack([res.results[c]["outp"] for c in range(NCORES)])


def kernel(**inputs):
    states = run_l0a(inputs)
    x1 = run_l0b(inputs, states)
    s1, xi1b = run_l1a(inputs, x1)
    out = run_l1b(inputs, x1, s1, xi1b)
    full = np.concatenate([out[c].T for c in range(NCORES)], axis=0)
    return np.ascontiguousarray(full.reshape(1, TFULL, DM), dtype=np.float32)


